# version 08
# baseline (speedup 1.0000x reference)
import contextlib
from types import SimpleNamespace as NS
import numpy as np
import concourse.bass as bass
import concourse.mybir as mybir

F32 = mybir.dt.float32
BF16 = mybir.dt.bfloat16
I32 = mybir.dt.int32
AF = mybir.ActivationFunctionType
ALU = mybir.AluOpType
AX = mybir.AxisListType

P = 128
D = 2048
NCH = 16
TOK = 1024
NTT = 8
T = 2048
EPS_LN = 1e-5
EPS_RMS = 1e-6
PIECES = [(0, 512), (512, 256)]


def chunks_of(qt):
    klen = (qt + 1) * P
    out = []
    kc = 0
    while kc * 512 < klen:
        out.append((kc, min(512, klen - kc * 512)))
        kc += 1
    return out


def attn_sched(npairs):
    sched = []
    c_smm = c_mask = c_exp = c_zsc = c_kt = c_oev = 0
    for p in range(npairs):
        for g in range(4):
            rec = {"smm0": c_smm, "mask0": c_mask, "exp0": c_exp,
                   "zsc0": c_zsc, "kt0": c_kt, "oev0": c_oev}
            for i in range(4):
                qt = 4 * g + i
                n = len(chunks_of(qt))
                c_smm += n
                c_exp += n
                c_mask += 1
                c_zsc += 1
            c_kt += 4 * g + 4
            c_oev += 1
            rec.update({"smm1": c_smm, "mask1": c_mask, "exp1": c_exp,
                        "zsc1": c_zsc, "kt1": c_kt, "oev1": c_oev})
            sched.append(rec)
    return sched


def build_fused():
    nc = bass.Bass("TRN2", target_bir_lowering=False)

    x_f_in = nc.dram_tensor("x_f", [D, TOK], F32, kind="ExternalInput")
    x_t_in = nc.dram_tensor("x_t", [TOK, D], F32, kind="ExternalInput")
    coreinfo = nc.dram_tensor("coreinfo", [1, 4], I32, kind="ExternalInput")
    w_attn = nc.dram_tensor("w_attn", [D, 3 * D], F32, kind="ExternalInput")
    b_attn = nc.dram_tensor("b_attn", [3 * D], F32, kind="ExternalInput")
    w_proj = nc.dram_tensor("w_proj", [D, D], F32, kind="ExternalInput")
    b_proj = nc.dram_tensor("b_proj", [D], F32, kind="ExternalInput")
    gate_w = nc.dram_tensor("gate_w", [D, 8], F32, kind="ExternalInput")
    w1 = nc.dram_tensor("w1", [8, D, 768], F32, kind="ExternalInput")
    w2 = nc.dram_tensor("w2", [8, 768, D], F32, kind="ExternalInput")
    w3 = nc.dram_tensor("w3", [8, D, 768], F32, kind="ExternalInput")
    WqT = nc.dram_tensor("WqT", [D, D], F32, kind="ExternalInput")
    WkT = nc.dram_tensor("WkT", [D, D], F32, kind="ExternalInput")
    WvT = nc.dram_tensor("WvT", [D, D], F32, kind="ExternalInput")
    WgkT = nc.dram_tensor("WgkT", [D, 32], F32, kind="ExternalInput")
    WgT = nc.dram_tensor("WgT", [D, D], F32, kind="ExternalInput")
    WoT = nc.dram_tensor("WoT", [D, D], F32, kind="ExternalInput")
    wqgT = nc.dram_tensor("wqgT", [D, 5632], I32, kind="ExternalInput")
    wquT = nc.dram_tensor("wquT", [D, 5632], I32, kind="ExternalInput")
    wqdT = nc.dram_tensor("wqdT", [5632, D], I32, kind="ExternalInput")
    sg2d = nc.dram_tensor("sg2d", [P, 44], F32, kind="ExternalInput")
    su2d = nc.dram_tensor("su2d", [P, 44], F32, kind="ExternalInput")
    sd2d = nc.dram_tensor("sd2d", [P, NCH], F32, kind="ExternalInput")
    rope_cos = nc.dram_tensor("rope_cos", [TOK, 64], F32, kind="ExternalInput")
    rope_sin = nc.dram_tensor("rope_sin", [TOK, 64], F32, kind="ExternalInput")
    id32_in = nc.dram_tensor("id32", [P, P], F32, kind="ExternalInput")
    idbf_in = nc.dram_tensor("idbf", [P, P], BF16, kind="ExternalInput")
    maskneg_in = nc.dram_tensor("maskneg", [P, P], F32, kind="ExternalInput")
    maskbf_in = nc.dram_tensor("maskbf", [P, P], BF16, kind="ExternalInput")
    y_out = nc.dram_tensor("y", [TOK, D], F32, kind="ExternalOutput")

    A0p_own = [nc.dram_tensor(f"A0own{i}", [TOK, D], BF16) for i in range(3)]
    A0p_o = [nc.dram_tensor(f"A0o{i}", [T, D], BF16) for i in range(3)]
    B0own = nc.dram_tensor("B0own", [TOK, D], BF16)
    B0o = nc.dram_tensor("B0o", [T, D], BF16)
    A1p_own = [nc.dram_tensor(f"A1own{i}", [TOK, D], BF16) for i in range(3)]
    A1p_o = [nc.dram_tensor(f"A1o{i}", [T, D], BF16) for i in range(3)]
    Agown = nc.dram_tensor("Agown", [TOK, 32], F32)
    Ago = nc.dram_tensor("Ago", [T, 32], F32)
    B1own = nc.dram_tensor("B1own", [TOK, D], BF16)
    B1o = nc.dram_tensor("B1o", [T, D], BF16)
    gdram = nc.dram_tensor("gdram", [32, T], F32)
    A0loc = nc.dram_tensor("A0loc", [T, 3 * TOK], BF16)
    A1loc = nc.dram_tensor("A1loc", [T, 3 * TOK], BF16)
    B1loc = nc.dram_tensor("B1loc", [T, TOK], BF16)
    gloc = nc.dram_tensor("gloc", [16, T], F32)
    RG = [[0, 1], [2, 3], [4, 5], [6, 7]]

    ctx = contextlib.ExitStack()
    E = ctx.enter_context

    stream = E(nc.sbuf_tensor("stream", [P, NTT * D], F32))
    x_f = E(nc.sbuf_tensor("x_fb", [P, NCH * TOK], BF16))
    isb = E(nc.sbuf_tensor("isb", [1, 4], I32))
    i32sb = E(nc.sbuf_tensor("i32sb", [P, P], F32))
    ibfsb = E(nc.sbuf_tensor("ibfsb", [P, P], BF16))
    mnegsb = E(nc.sbuf_tensor("mnegsb", [P, P], F32))
    mbfsb = E(nc.sbuf_tensor("mbfsb", [P, P], BF16))
    ones1 = E(nc.sbuf_tensor("ones1", [1, P], BF16))
    onec32 = E(nc.sbuf_tensor("onec32", [1, P], F32))
    nones = E(nc.sbuf_tensor("nones", [P, 1], BF16))
    epsln = E(nc.sbuf_tensor("epsln", [P, 1], F32))
    epsrm = E(nc.sbuf_tensor("epsrm", [P, 1], F32))
    shcol = E(nc.sbuf_tensor("shcol", [P, 1], F32))
    sq = E(nc.sbuf_tensor("sq", [P, NTT], F32))
    sums = E(nc.sbuf_tensor("sums", [P, NTT], F32))
    mu = E(nc.sbuf_tensor("mu", [P, NTT], F32))
    varv = E(nc.sbuf_tensor("varv", [P, NTT], F32))
    rsd = E(nc.sbuf_tensor("rsd", [P, NTT], F32))
    scr = E(nc.sbuf_tensor("scr", [P, D], F32))
    sgc_p = E(nc.sbuf_tensor("sgc_p", [P, 44], F32))
    suc_p = E(nc.sbuf_tensor("suc_p", [P, 44], F32))
    sdc_p = E(nc.sbuf_tensor("sdc_p", [P, NCH], F32))

    ccp = [E(nc.semaphore(f"ccp{i}")) for i in range(3)]
    cpp = [E(nc.semaphore(f"cpp{i}")) for i in range(3)]
    cc1 = E(nc.semaphore("cc1"))
    cc3 = E(nc.semaphore("cc3"))
    cst = E(nc.semaphore("cst"))

    sc = contextlib.ExitStack()
    p1 = NS()
    p1.wblk = [sc.enter_context(nc.sbuf_tensor(f"p1_wblk{i}", [P, NCH * 512], BF16)) for i in range(2)]
    p1.ev = [sc.enter_context(nc.sbuf_tensor(f"p1_ev{i}", [P, NTT * 512], BF16)) for i in range(2)]
    p1.brow = sc.enter_context(nc.sbuf_tensor("p1_brow", [1, 3 * D], BF16))
    p1.ps = [sc.enter_context(nc.psum_tensor(f"p1_ps{i}", [P, 512], F32)) for i in range(2)]
    _s = E(nc.semaphore("p1_ldw"))
    p1.ldw = [_s, _s]
    p1.ldx = E(nc.semaphore("p1_ldx"))
    p1.mm = E(nc.semaphore("p1_mm"))
    p1.evs = E(nc.semaphore("p1_evs"))
    _s = E(nc.semaphore("p1_so"))
    p1.so = [_s, _s]
    sc.close()
    sc = contextlib.ExitStack()
    p2 = NS()
    p2.qsb = [sc.enter_context(nc.sbuf_tensor(f"p2_qsb{i}", [P, T], BF16)) for i in range(2)]
    p2.ksb = [sc.enter_context(nc.sbuf_tensor(f"p2_ksb{i}", [P, T], BF16)) for i in range(2)]
    p2.vsb = [sc.enter_context(nc.sbuf_tensor(f"p2_vsb{i}", [P, 16 * P], BF16)) for i in range(2)]
    p2.psb = [sc.enter_context(nc.sbuf_tensor(f"p2_psb{i}", [P, 4 * T], BF16)) for i in range(2)]
    p2.ptsb = [sc.enter_context(nc.sbuf_tensor(f"p2_ptsb{i}", [P, 512], BF16)) for i in range(2)]
    p2.zp = sc.enter_context(nc.sbuf_tensor("p2_zp", [P, 16], F32))
    p2.zinv = sc.enter_context(nc.sbuf_tensor("p2_zinv", [P, 4], F32))
    p2.osb = [sc.enter_context(nc.sbuf_tensor(f"p2_osb{i}", [P, T], BF16)) for i in range(2)]
    p2.ps_s = [sc.enter_context(nc.psum_tensor(f"p2_ps_s{i}", [P, 512], F32)) for i in range(4)]
    p2.pt_ps = [sc.enter_context(nc.psum_tensor(f"p2_pt_ps{i}", [P, 1024], BF16)) for i in range(2)]
    p2.o_ps = sc.enter_context(nc.psum_tensor("p2_o_ps", [P, 512], F32))
    for nm in ["qld", "vld", "ssem", "msem", "esem", "zsem", "tsem", "ptsem",
               "avsem", "osem", "vself"]:
        setattr(p2, nm, E(nc.semaphore(f"p2_{nm}")))
    _s = E(nc.semaphore("p2_sto"))
    p2.sto = [_s, _s]
    p2.sched = attn_sched(8)
    sc.close()
    sc = contextlib.ExitStack()
    p3 = NS()
    p3.gwsb = sc.enter_context(nc.sbuf_tensor("p3_gwsb", [P, NCH * 8], BF16))
    p3.bprow = sc.enter_context(nc.sbuf_tensor("p3_bprow", [1, D], BF16))
    p3.gexp = sc.enter_context(nc.sbuf_tensor("p3_gexp", [P, NTT * 8], F32))
    p3.comb = sc.enter_context(nc.sbuf_tensor("p3_comb", [P, NTT * 8], F32))
    p3.e8a = sc.enter_context(nc.sbuf_tensor("p3_e8a", [P, 8], F32))
    p3.t2a = sc.enter_context(nc.sbuf_tensor("p3_t2a", [P, 1], F32))
    p3.t2b = sc.enter_context(nc.sbuf_tensor("p3_t2b", [P, 1], F32))
    scb = contextlib.ExitStack()
    p3.ofb = scb.enter_context(nc.sbuf_tensor("p3_ofb", [P, NCH * TOK], BF16))
    p3.wpb = [scb.enter_context(nc.sbuf_tensor(f"p3_wpb{i}", [P, NCH * 512], BF16)) for i in range(2)]
    scb.close()
    p3.w13b = [sc.enter_context(nc.sbuf_tensor(f"p3_w13b{i}", [P, NCH * 512], BF16)) for i in range(4)]
    p3.w2b = [sc.enter_context(nc.sbuf_tensor(f"p3_w2b{i}", [P, 6 * 512], BF16)) for i in range(2)]
    p3.gs = [sc.enter_context(nc.sbuf_tensor(f"p3_gs{i}", [P, 512], F32)) for i in range(2)]
    p3.a2t = [sc.enter_context(nc.sbuf_tensor(f"p3_a2t{i}", [P, 512], BF16)) for i in range(2)]
    p3.a2f = sc.enter_context(nc.sbuf_tensor("p3_a2f", [P, NTT * 6 * P], BF16))
    p3.ps = [sc.enter_context(nc.psum_tensor(f"p3_ps{i}", [P, 512], F32)) for i in range(2)]
    p3.pg = [sc.enter_context(nc.psum_tensor(f"p3_pg{i}", [P, 512], F32)) for i in range(2)]
    p3.pu = [sc.enter_context(nc.psum_tensor(f"p3_pu{i}", [P, 512], F32)) for i in range(2)]
    p3.ptr = [sc.enter_context(nc.psum_tensor(f"p3_ptr{i}", [P, 1024], BF16)) for i in range(2)]
    p3.pxf = p3.pu
    for nm in ["ldx", "ldc", "mm", "evs", "lnv", "lns", "trs", "xfs", "gsem",
               "vch", "sil", "a2s", "a2e", "dmm", "dev"]:
        setattr(p3, nm, E(nc.semaphore(f"p3_{nm}")))
    _s = E(nc.semaphore("p3_ldw")); p3.ldw = [_s, _s]
    _s = E(nc.semaphore("p3_lde")); p3.lde = [_s, _s]
    _s = E(nc.semaphore("p3_ld2")); p3.ld2 = [_s, _s]
    sc.close()
    sc = contextlib.ExitStack()
    p4 = NS()
    p4.wblk = [sc.enter_context(nc.sbuf_tensor(f"p4_wblk{i}", [P, NCH * 512], BF16)) for i in range(2)]
    p4.wgk = sc.enter_context(nc.sbuf_tensor("p4_wgk", [P, NCH * 32], BF16))
    p4.stage = sc.enter_context(nc.sbuf_tensor("p4_stage", [P, NTT * D], BF16))
    p4.cosb = sc.enter_context(nc.sbuf_tensor("p4_cosb", [P, NTT * 64], BF16))
    p4.sinb = sc.enter_context(nc.sbuf_tensor("p4_sinb", [P, NTT * 64], BF16))
    p4.tmp = [sc.enter_context(nc.sbuf_tensor(f"p4_tmp{i}", [P, 1024], BF16)) for i in range(4)]
    p4.gks = sc.enter_context(nc.sbuf_tensor("p4_gks", [P, NTT * 32], F32))
    p4.ps = [sc.enter_context(nc.psum_tensor(f"p4_ps{i}", [P, 512], F32)) for i in range(2)]
    p4.pgk = sc.enter_context(nc.psum_tensor("p4_pgk", [P, 32], F32))
    for nm in ["ldc", "mm", "mgk", "esem", "gsig", "vch", "sto", "trs", "xfs"]:
        setattr(p4, nm, E(nc.semaphore(f"p4_{nm}")))
    _s = E(nc.semaphore("p4_ldw")); p4.ldw = [_s, _s]
    sc.close()
    sc = contextlib.ExitStack()
    p5 = NS()
    HD5 = 64
    p5.qsb = [sc.enter_context(nc.sbuf_tensor(f"p5_qsb{i}", [P, T], BF16)) for i in range(2)]
    p5.ksb = [sc.enter_context(nc.sbuf_tensor(f"p5_ksb{i}", [P, T], BF16)) for i in range(2)]
    p5.vsb = [sc.enter_context(nc.sbuf_tensor(f"p5_vsb{i}", [P, 16 * P], BF16)) for i in range(2)]
    p5.gsb = sc.enter_context(nc.sbuf_tensor("p5_gsb", [P, 16 * 32], F32))
    p5.gz = sc.enter_context(nc.sbuf_tensor("p5_gz", [32, 1], F32))
    p5.grow = [sc.enter_context(nc.sbuf_tensor(f"p5_grow{i}", [1, T], F32)) for i in range(2)]
    p5.gq16 = sc.enter_context(nc.sbuf_tensor("p5_gq16", [P, 16], F32))
    p5.gkbc = sc.enter_context(nc.sbuf_tensor("p5_gkbc", [P, T], F32))
    p5.esb = [sc.enter_context(nc.sbuf_tensor(f"p5_esb{i}", [P, 512], BF16)) for i in range(2)]
    p5.ptsb = [sc.enter_context(nc.sbuf_tensor(f"p5_ptsb{i}", [P, 512], BF16)) for i in range(2)]
    p5.osb = [sc.enter_context(nc.sbuf_tensor(f"p5_osb{i}", [HD5, T], BF16)) for i in range(2)]
    _scg = contextlib.ExitStack()
    p5.gf = _scg.enter_context(nc.sbuf_tensor("p5_gf", [32, T], F32))
    p5.gcum = _scg.enter_context(nc.sbuf_tensor("p5_gcum", [32, T], F32))
    _scg.close()
    p5.asb = [sc.enter_context(nc.sbuf_tensor(f"p5_asb{i}", [P, 4 * T], BF16)) for i in range(2)]
    p5.ps_s = [sc.enter_context(nc.psum_tensor(f"p5_ps_s{i}", [P, 512], F32)) for i in range(4)]
    p5.pt_ps = [sc.enter_context(nc.psum_tensor(f"p5_pt_ps{i}", [P, 1024], BF16)) for i in range(2)]
    p5.o_ps = sc.enter_context(nc.psum_tensor("p5_o_ps", [P, 512], F32))
    p5.pbc = sc.enter_context(nc.psum_tensor("p5_pbc", [P, 512], F32))
    for nm in ["qld", "gsc", "gtr", "gqt", "gqe", "bcs", "bce", "ssem", "esem",
               "asem", "tsem", "ptsem", "ptsem_s", "avsem", "osem", "gst", "vsf", "gqs"]:
        setattr(p5, nm, E(nc.semaphore(f"p5_{nm}")))
    _s = E(nc.semaphore("p5_gld")); p5.gld = [_s, _s]
    _s = E(nc.semaphore("p5_sto")); p5.sto = [_s, _s]
    p5.sched = attn_sched(16)
    sc.close()
    sc = contextlib.ExitStack()
    p6 = NS()
    NHC = 44
    p6.sgc = sgc_p
    p6.suc = suc_p
    p6.sdc = sdc_p
    p6.hvar = sc.enter_context(nc.sbuf_tensor("p6_hvar", [P, 32], F32))
    p6.rsrow = sc.enter_context(nc.sbuf_tensor("p6_rsrow", [1, TOK], BF16))
    p6.csrow = sc.enter_context(nc.sbuf_tensor("p6_csrow", [1, TOK], BF16))
    scb = contextlib.ExitStack()
    p6.ogf = scb.enter_context(nc.sbuf_tensor("p6_ogf", [P, NCH * TOK], BF16))
    p6.sg_sb = scb.enter_context(nc.sbuf_tensor("p6_sg_sb", [P, NTT * D], BF16))
    p6.otb = [scb.enter_context(nc.sbuf_tensor(f"p6_otb{i}", [P, D], BF16)) for i in range(2)]
    p6.wb = [scb.enter_context(nc.sbuf_tensor(f"p6_wb{i}", [P, NCH * 256], BF16)) for i in range(2)]
    scb.close()
    p6.a2sb = sc.enter_context(nc.sbuf_tensor("p6_a2sb", [P, NHC * 512], BF16))
    p6.wqg_b = [sc.enter_context(nc.sbuf_tensor(f"p6_wqg_b{i}", [P, NCH * P], BF16)) for i in range(2)]
    p6.wqu_b = [sc.enter_context(nc.sbuf_tensor(f"p6_wqu_b{i}", [P, NCH * P], BF16)) for i in range(2)]
    p6.wqd_b = [sc.enter_context(nc.sbuf_tensor(f"p6_wqd_b{i}", [P, NHC * P], BF16)) for i in range(2)]
    p6.sgh = [sc.enter_context(nc.sbuf_tensor(f"p6_sgh{i}", [P, 512], BF16)) for i in range(2)]
    p6.yfb = [sc.enter_context(nc.sbuf_tensor(f"p6_yfb{i}", [P, 512], BF16)) for i in range(2)]
    p6.ps = [sc.enter_context(nc.psum_tensor(f"p6_ps{i}", [P, 512], F32)) for i in range(2)]
    p6.pu = [sc.enter_context(nc.psum_tensor(f"p6_pu{i}", [P, 512], F32)) for i in range(2)]
    p6.ptrb = [sc.enter_context(nc.psum_tensor(f"p6_ptr{i}", [P, 1024], BF16)) for i in range(2)]
    p6.prow = sc.enter_context(nc.psum_tensor("p6_prow", [P, 512], F32))
    for nm in ["ldm", "mm", "evs", "lnv", "lns", "onr", "sil", "trs", "xfs",
               "rsm", "rse", "gup", "a2s", "dmm", "dse", "dtr", "dev", "sto"]:
        setattr(p6, nm, E(nc.semaphore(f"p6_{nm}")))
    _s = E(nc.semaphore("p6_ldo")); p6.ldo = [_s, _s]
    _s = E(nc.semaphore("p6_ldw")); p6.ldw = [_s, _s]
    _s = E(nc.semaphore("p6_ldq")); p6.ldq = [_s, _s]
    _s = E(nc.semaphore("p6_ldd")); p6.ldd = [_s, _s]
    sc.close()

    NGP6 = 64
    NWO6 = 64

    def mm_gu3(gidx):
        return 40 + gidx + 1

    NHC = 44
    with nc.Block() as block:

        @block.sync
        def _(sy):
            sy.dma_start(out=isb[:, :], in_=coreinfo[:, :]).then_inc(cst, 16)
            sy.dma_start(out=i32sb[:, :], in_=id32_in[:, :]).then_inc(cst, 16)
            sy.dma_start(out=ibfsb[:, :], in_=idbf_in[:, :]).then_inc(cst, 16)
            sy.dma_start(out=mnegsb[:, :], in_=maskneg_in[:, :]).then_inc(cst, 16)
            sy.dma_start(out=mbfsb[:, :], in_=maskbf_in[:, :]).then_inc(cst, 16)
            sy.dma_start(out=stream[:, :].rearrange("p (tt d) -> p tt d", tt=NTT),
                         in_=x_t_in.ap().rearrange("(tt p) d -> p tt d", p=P)
                         ).then_inc(p3.ldx, 16)
            sy.dma_start(out=p6.sgc[:, :], in_=sg2d[:, :]).then_inc(p6.ldm, 16)
            sy.dma_start(out=p6.suc[:, :], in_=su2d[:, :]).then_inc(p6.ldm, 16)
            sy.dma_start(out=p6.sdc[:, :], in_=sd2d[:, :]).then_inc(p6.ldm, 16)
            sy.wait_ge(cst, 81)
            rhb0 = sy.alloc_register("rhb0")
            rhb1 = sy.alloc_register("rhb1")
            rtok = sy.alloc_register("rtok")
            sy.load(rhb0, isb[0:1, 0:1])
            sy.load(rhb1, isb[0:1, 1:2])
            sy.load(rtok, isb[0:1, 2:3])
            vhb0 = sy.snap(rhb0, min_val=0, max_val=1024)
            vhb1 = sy.snap(rhb1, min_val=0, max_val=1024)
            vtok = sy.snap(rtok, min_val=0, max_val=1024)

            A0rs = [t_.ap().rearrange("(tt p) o -> p tt o", p=P) for t_ in A0p_own]
            for ob in range(12):
                sy.wait_ge(p1.evs, 8 * (ob + 1))
                if ob >= 1:
                    sy.wait_ge(p1.so[0], 16 * ob)
                sy.dma_start(out=A0rs[ob // 4][:, :, (ob % 4) * 512:(ob % 4 + 1) * 512],
                             in_=p1.ev[ob % 2][:, :].rearrange("p (tt o) -> p tt o", tt=NTT)
                             ).then_inc(p1.so[0], 16)

            for i in range(3):
                sy.wait_ge(ccp[i], 1)
                sy.dma_start(out=A0loc[:, i * TOK:(i + 1) * TOK],
                             in_=A0p_o[i][:, bass.ds(vhb0, TOK)]).then_inc(cpp[i], 16)
            for h in range(8):
                if h >= 1:
                    sy.wait_ge(p2.qld, 32 * h)
                if h >= 2:
                    sy.wait_ge(p2.avsem, p2.sched[(h - 2) * 4 + 3]["kt1"])
                else:
                    sy.wait_ge(cpp[0], 16)
                sy.dma_start(out=p2.qsb[h % 2][:, :],
                             in_=A0loc[:, h * P:(h + 1) * P],
                             transpose=True).then_inc(p2.qld, 16)
                if h < 2:
                    sy.wait_ge(cpp[1], 16)
                sy.dma_start(out=p2.ksb[h % 2][:, :],
                             in_=A0loc[:, TOK + h * P:TOK + (h + 1) * P],
                             transpose=True).then_inc(p2.qld, 16)
                if h < 2:
                    sy.wait_ge(cpp[2], 16)
                if h >= 1:
                    sy.wait_ge(p2.vld, 16 * h)
                sy.dma_start(out=p2.vsb[h % 2][:, :].rearrange("p (kt d) -> p kt d", kt=16),
                             in_=A0loc.ap().rearrange("(kt p) o -> p kt o", p=P)
                             [:, :, 2 * TOK + h * P:2 * TOK + (h + 1) * P]
                             ).then_inc(p2.vld, 16)

            sy.wait_ge(cc1, 1)
            sy.dma_start(out=p3.ofb[:, :].rearrange("p (c t) -> p c t", c=NCH),
                         in_=B0o.ap().rearrange("(c p) t -> p c t", p=P)
                         [:, :, bass.ds(vtok, TOK)]).then_inc(p3.ldx, 16)

            sy.wait_ge(p4.gsig, 2 * NTT)
            sy.dma_start(out=Agown.ap().rearrange("(tt p) e -> p tt e", p=P),
                         in_=p4.gks[:, :].rearrange("p (tt e) -> p tt e", tt=NTT)
                         ).then_inc(p4.sto, 16)
            A1rs = [t_.ap().rearrange("(tt p) o -> p tt o", p=P) for t_ in A1p_own]
            for mi in range(3):
                if mi < 2:
                    sy.wait_ge(p4.vch, 48 * (mi + 1))
                else:
                    sy.wait_ge(p4.esem, 96)
                sy.wait_ge(p4.sto, 16 * (mi + 1))
                sy.dma_start(out=A1rs[mi][:, :, :],
                             in_=p4.stage[:, :].rearrange("p (tt o) -> p tt o", tt=NTT)
                             ).then_inc(p4.sto, 16)

            for i in range(3):
                sy.wait_ge(ccp[i], 2)
                sy.dma_start(out=A1loc[:, i * TOK:(i + 1) * TOK],
                             in_=A1p_o[i][:, bass.ds(vhb1, TOK)]).then_inc(cpp[i], 16)
            for prr in range(8):
                if prr >= 1:
                    sy.wait_ge(p5.qld, 48 * prr)
                if prr >= 2:
                    sy.wait_ge(p5.avsem, p5.sched[(2 * (prr - 2) + 1) * 4 + 3]["kt1"])
                else:
                    sy.wait_ge(cpp[0], 32)
                sy.dma_start(out=p5.qsb[prr % 2][:, :],
                             in_=A1loc[:, prr * P:(prr + 1) * P],
                             transpose=True).then_inc(p5.qld, 16)
                if prr < 2:
                    sy.wait_ge(cpp[1], 32)
                sy.dma_start(out=p5.ksb[prr % 2][:, :],
                             in_=A1loc[:, TOK + prr * P:TOK + (prr + 1) * P],
                             transpose=True).then_inc(p5.qld, 16)
                if prr < 2:
                    sy.wait_ge(cpp[2], 32)
                sy.dma_start(out=p5.vsb[prr % 2][:, :].rearrange("p (kt d) -> p kt d", kt=16),
                             in_=A1loc.ap().rearrange("(kt p) o -> p kt o", p=P)
                             [:, :, 2 * TOK + prr * P:2 * TOK + (prr + 1) * P]
                             ).then_inc(p5.qld, 16)

            sy.wait_ge(cc3, 1)
            sy.dma_start(out=B1loc[:, :],
                         in_=B1o[:, bass.ds(vtok, TOK)]).then_inc(cc3, 16)
            for tt in range(NTT):
                if tt >= 1:
                    sy.wait_ge(p6.ldo[0], 16 * tt)
                if tt >= 2:
                    sy.wait_ge(p6.onr, 5 * (tt - 1))
                else:
                    sy.wait_ge(cc3, 17)
                sy.dma_start(out=p6.otb[tt % 2][:, :],
                             in_=B1loc[:, tt * P:(tt + 1) * P],
                             transpose=True).then_inc(p6.ldo[0], 16)
            for tt in range(NTT):
                sy.wait_ge(p6.lnv, 56 + 7 * (tt + 1))
                sy.dma_start(out=y_out.ap().rearrange("(tt p) d -> p tt d", p=P)[:, tt, :],
                             in_=stream[:, tt * D:(tt + 1) * D]).then_inc(p6.sto, 16)
            sy.wait_ge(p6.sto, 16 * NTT)

        @block.gpsimd
        def _(g):
            g.memset(ones1[:, :], 1.0)
            g.memset(onec32[:, :], 1.0)
            g.memset(nones[:, :], -128.0)
            g.memset(epsln[:, :], EPS_LN)
            g.memset(epsrm[:, :], EPS_RMS)
            g.memset(p5.gz[:, :], 0.0)
            g.memset(shcol[:, :], -16.0).then_inc(cst, 1)
            g.dma_start(out=x_f[:, :].rearrange("p (c t) -> p c t", c=NCH),
                        in_=x_f_in.ap().rearrange("(c p) t -> p c t", p=P)
                        ).then_inc(p1.ldx, 16)
            g.dma_start(out=p1.brow[:, :], in_=b_attn.ap().unsqueeze(0)).then_inc(p1.ldx, 16)
            for ob in range(12):
                if ob >= 1:
                    g.wait_ge(p1.ldw[0], 16 * ob)
                if ob >= 2:
                    g.wait_ge(p1.mm, 8 * (ob - 1))
                g.dma_start(out=p1.wblk[ob % 2][:, :].rearrange("p (c o) -> p c o", c=NCH),
                            in_=w_attn.ap().rearrange("(c p) o -> p c o", p=P)
                            [:, :, ob * 512:(ob + 1) * 512]).then_inc(p1.ldw[0], 16)
                if ob == 5 or ob == 9:
                    i = (ob - 5) // 4
                    g.wait_ge(p1.so[0], 16 * 4 * (i + 1))
                    g.collective_compute("AllGather", ALU.bypass, replica_groups=RG,
                                         ins=[A0p_own[i].ap().opt()],
                                         outs=[A0p_o[i].ap().opt()]
                                         ).then_inc(ccp[i], 1)
            g.wait_ge(p1.so[0], 16 * 12)
            g.collective_compute("AllGather", ALU.bypass, replica_groups=RG,
                                 ins=[A0p_own[2].ap().opt()], outs=[A0p_o[2].ap().opt()]
                                 ).then_inc(ccp[2], 1)
            for h in range(8):
                g.wait_ge(p2.osem, 4 * (h + 1))
                if h >= 1:
                    g.wait_ge(p2.sto[0], 16 * h)
                g.dma_start(out=B0own[h * P:(h + 1) * P, :],
                            in_=p2.osb[h % 2][:, :]).then_inc(p2.sto[0], 16)
            g.wait_ge(p2.sto[0], 16 * 8)
            g.collective_compute("AllGather", ALU.bypass, replica_groups=RG,
                                 ins=[B0own.ap().opt()], outs=[B0o.ap().opt()]
                                 ).then_inc(cc1, 1)
            g.wait_ge(cc1, 1)
            g.dma_start(out=p3.bprow[:, :], in_=b_proj.ap().unsqueeze(0)).then_inc(p3.ldc, 16)
            g.dma_start(out=p3.gwsb[:, :].rearrange("p (c e) -> p c e", c=NCH),
                        in_=gate_w.ap().rearrange("(c p) e -> p c e", p=P)).then_inc(p3.ldc, 16)
            for ob in range(4):
                if ob >= 1:
                    g.wait_ge(p3.ldw[0], 16 * ob)
                if ob >= 2:
                    g.wait_ge(p3.mm, 8 * (ob - 1))
                g.dma_start(out=p3.wpb[ob % 2][:, :].rearrange("p (c o) -> p c o", c=NCH),
                            in_=w_proj.ap().rearrange("(c p) o -> p c o", p=P)
                            [:, :, ob * 512:(ob + 1) * 512]).then_inc(p3.ldw[0], 16)
            g.wait_ge(p3.mm, 32)
            for e in range(8):
                for piece, (foff, fw) in enumerate(PIECES):
                    pig = e * 2 + piece
                    par = pig % 2
                    if pig >= 1:
                        g.wait_ge(p3.lde[0], 32 * pig)
                    for (mat, slot) in [(w1, 2 * par), (w3, 2 * par + 1)]:
                        if pig >= 2:
                            pe_, pp_ = divmod(pig - 2, 2)
                            g.wait_ge(p3.mm, mm_gu3(pe_ * 16 + pp_ * 8 + 7))
                        g.dma_start(
                            out=p3.w13b[slot][:, 0:NCH * fw].rearrange(
                                "p (c f) -> p c f", c=NCH),
                            in_=mat[e, :, foff:foff + fw].rearrange("(c p) f -> p c f", p=P)
                        ).then_inc(p3.lde[0], 16)
                for db in range(4):
                    j = e * 4 + db
                    if j >= 1:
                        g.wait_ge(p3.ld2[0], 16 * j)
                    if j >= 2:
                        g.wait_ge(p3.dmm, 48 * (j - 1))
                    g.dma_start(
                        out=p3.w2b[j % 2][:, :].rearrange("p (c d) -> p c d", c=6),
                        in_=w2[e, :, db * 512:(db + 1) * 512].rearrange("(c p) d -> p c d", p=P)
                    ).then_inc(p3.ld2[0], 16)
            g.wait_ge(p3.lnv, 112)
            g.dma_start(out=p4.wgk[:, :].rearrange("p (c e) -> p c e", c=NCH),
                        in_=WgkT.ap().rearrange("(c p) e -> p c e", p=P)).then_inc(p4.ldc, 16)
            g.dma_start(out=p4.cosb[:, :].rearrange("p (tt f) -> p tt f", tt=NTT),
                        in_=rope_cos.ap().rearrange("(tt p) f -> p tt f", p=P)
                        ).then_inc(p4.ldc, 16)
            g.dma_start(out=p4.sinb[:, :].rearrange("p (tt f) -> p tt f", tt=NTT),
                        in_=rope_sin.ap().rearrange("(tt p) f -> p tt f", p=P)
                        ).then_inc(p4.ldc, 16)
            bi = 0
            for mi, mat in enumerate([WqT, WkT, WvT]):
                for ob in range(4):
                    if bi >= 1:
                        g.wait_ge(p4.ldw[0], 16 * bi)
                    if bi >= 2:
                        g.wait_ge(p4.mm, 8 * (bi - 1))
                    g.dma_start(out=p4.wblk[bi % 2][:, :].rearrange("p (c o) -> p c o", c=NCH),
                                in_=mat.ap().rearrange("(c p) o -> p c o", p=P)
                                [:, :, ob * 512:(ob + 1) * 512]).then_inc(p4.ldw[0], 16)
                    if bi == 5:
                        g.wait_ge(p4.sto, 16)
                        g.collective_compute("AllGather", ALU.bypass, replica_groups=RG,
                                             ins=[Agown.ap().opt()], outs=[Ago.ap().opt()]
                                             ).then_inc(cc1, 1)
                    if bi == 7 or bi == 9:
                        i = (bi - 7) // 2
                        g.wait_ge(p4.sto, 16 * (i + 2))
                        g.collective_compute("AllGather", ALU.bypass, replica_groups=RG,
                                             ins=[A1p_own[i].ap().opt()],
                                             outs=[A1p_o[i].ap().opt()]
                                             ).then_inc(ccp[i], 1)
                    bi += 1
            g.wait_ge(p4.sto, 64)
            g.collective_compute("AllGather", ALU.bypass, replica_groups=RG,
                                 ins=[A1p_own[2].ap().opt()], outs=[A1p_o[2].ap().opt()]
                                 ).then_inc(ccp[2], 1)
            g.wait_ge(cc1, 2)
            g.dma_start(out=p5.gsb[:, :].rearrange("p (tt h) -> p tt h", tt=16),
                        in_=Ago.ap().rearrange("(tt p) h -> p tt h", p=P)
                        ).then_inc(p5.gqs, 16)
            g.wait_ge(p5.gsc, 1)
            g.dma_start(out=gdram[:, :], in_=p5.gcum[:, :]).then_inc(p5.gst, 16)
            g.wait_ge(p5.gst, 16)
            rp5 = g.alloc_register("rp5")
            g.load(rp5, isb[0:1, 3:4])
            offv = g.snap(rp5, min_val=0, max_val=16)
            g.dma_start(out=gloc[:, :],
                        in_=gdram[bass.ds(offv, 16), :]).then_inc(p5.gst, 16)
            g.wait_ge(p5.gst, 32)

            def p5_store(pr):
                g.wait_ge(p5.osem, 4 * (pr + 1))
                if pr >= 1:
                    g.wait_ge(p5.sto[0], 16 * pr)
                g.dma_start(out=B1own[pr * 64:(pr + 1) * 64, :],
                            in_=p5.osb[pr % 2][:, :]).then_inc(p5.sto[0], 16)

            for pr in range(16):
                if pr >= 1:
                    g.wait_ge(p5.gld[0], 16 * pr)
                if pr >= 2:
                    g.wait_ge(p5.bcs, 4 * (pr - 1))
                g.dma_start(out=p5.grow[pr % 2][:, :],
                            in_=gloc[pr:pr + 1, :]).then_inc(p5.gld[0], 16)
                if pr >= 2:
                    p5_store(pr - 2)
            p5_store(14)
            p5_store(15)
            g.wait_ge(p5.sto[0], 16 * 16)
            g.collective_compute("AllGather", ALU.bypass, replica_groups=RG,
                                 ins=[B1own.ap().opt()], outs=[B1o.ap().opt()]
                                 ).then_inc(cc3, 1)
            g.wait_ge(cc3, 1)
            bi = 0
            for mat in (WgT, WoT):
                for ob in range(8):
                    if bi >= 1:
                        g.wait_ge(p6.ldw[0], 16 * bi)
                    if bi >= 2:
                        g.wait_ge(p6.mm, 8 * (bi - 1))
                    g.dma_start(out=p6.wb[bi % 2][:, :].rearrange("p (c o) -> p c o", c=NCH),
                                in_=mat.ap().rearrange("(c p) o -> p c o", p=P)
                                [:, :, ob * 256:(ob + 1) * 256]).then_inc(p6.ldw[0], 16)
                    bi += 1
            qi = 0
            for th in range(2):
                for hc in range(NHC):
                    if qi >= 1:
                        g.wait_ge(p6.ldq[0], 32 * qi)
                    if qi >= 2:
                        g.wait_ge(p6.gup, 2 * (qi - 1))
                    else:
                        g.wait_ge(p6.mm, NGP6 + NWO6)
                    g.dma_start(out=p6.wqg_b[qi % 2][:, :].rearrange("p (c h) -> p c h", c=NCH),
                                in_=wqgT.ap().rearrange("(c p) h -> p c h", p=P)
                                [:, :, hc * P:(hc + 1) * P]).then_inc(p6.ldq[0], 16)
                    g.dma_start(out=p6.wqu_b[qi % 2][:, :].rearrange("p (c h) -> p c h", c=NCH),
                                in_=wquT.ap().rearrange("(c p) h -> p c h", p=P)
                                [:, :, hc * P:(hc + 1) * P]).then_inc(p6.ldq[0], 16)
                    qi += 1
                for dc in range(NCH):
                    di = th * NCH + dc
                    if di >= 1:
                        g.wait_ge(p6.ldd[0], 16 * di)
                    if di >= 2:
                        g.wait_ge(p6.dse, di - 1)
                    else:
                        g.wait_ge(p6.mm, NGP6 + NWO6)
                    g.dma_start(out=p6.wqd_b[di % 2][:, :].rearrange("p (c d) -> p c d", c=NHC),
                                in_=wqdT.ap().rearrange("(c p) d -> p c d", p=P)
                                [:, :, dc * P:(dc + 1) * P]).then_inc(p6.ldd[0], 16)

        @block.tensor
        def _(t):
            t.wait_ge(cst, 81)
            t.wait_ge(p1.ldx, 32)
            gi = 0
            for ob in range(12):
                t.wait_ge(p1.ldw[0], 16 * (ob + 1))
                for tt in range(NTT):
                    pb = p1.ps[gi % 2]
                    if gi >= 2:
                        t.wait_ge(p1.evs, gi - 1)
                    for c in range(NCH):
                        t.matmul(pb[:, :],
                                 x_f[:, c * TOK + tt * P: c * TOK + (tt + 1) * P],
                                 p1.wblk[ob % 2][:, c * 512:(c + 1) * 512],
                                 start=(c == 0), stop=False)
                    t.matmul(pb[:, :], ones1[:, :],
                             p1.brow[0:1, ob * 512:(ob + 1) * 512],
                             start=False, stop=True).then_inc(p1.mm, 1)
                    gi += 1
            def p2_smms(idx):
                h, gq = divmod(idx, 4)
                rec = p2.sched[idx]
                smm = rec["smm0"]
                for i in range(4):
                    qt = 4 * gq + i
                    for (kc, cw) in chunks_of(qt):
                        if smm >= 4:
                            t.wait_ge(p2.esem, smm - 3)
                        t.matmul(p2.ps_s[smm % 4][:, 0:cw],
                                 p2.qsb[h % 2][:, qt * P:(qt + 1) * P],
                                 p2.ksb[h % 2][:, kc * 512:kc * 512 + cw],
                                 start=True, stop=True).then_inc(p2.ssem, 1)
                        smm += 1

            def p2_tav(idx):
                h, gq = divmod(idx, 4)
                rec = p2.sched[idx]
                t.wait_ge(p2.vld, 16 * (h + 1))
                t.wait_ge(p2.zsem, rec["zsc1"])
                nkt = 4 * gq + 4

                def p2_tr(kt):
                    ktc = rec["kt0"] + kt
                    i0 = max(0, kt - 4 * gq)
                    if ktc >= 2:
                        t.wait_ge(p2.ptsem, ktc - 1)
                    last = None
                    for i in range(i0, 4):
                        last = t.transpose(p2.pt_ps[ktc % 2][:, i * P:(i + 1) * P],
                                           p2.psb[idx % 2][:, i * T + kt * P: i * T + (kt + 1) * P],
                                           ibfsb[:, :])
                    last.then_inc(p2.tsem, 1)

                def p2_av(kt):
                    ktc = rec["kt0"] + kt
                    i0 = max(0, kt - 4 * gq)
                    t.wait_ge(p2.ptsem, ktc + 1)
                    if kt == 0 and rec["oev0"] > 0:
                        t.wait_ge(p2.osem, rec["oev0"])
                    t.matmul(p2.o_ps[:, i0 * P:512],
                             p2.vsb[h % 2][:, kt * P:(kt + 1) * P],
                             p2.ptsb[ktc % 2][:, i0 * P:512],
                             start=(kt == 0), stop=(kt == nkt - 1)
                             ).then_inc(p2.avsem, 1)

                p2_tr(0)
                for kt in range(1, nkt):
                    p2_tr(kt)
                    p2_av(kt - 1)
                p2_av(nkt - 1)

            for idx in range(32):
                if idx % 4 == 0:
                    t.wait_ge(p2.qld, 32 * (idx // 4 + 1))
                p2_smms(idx)
                if idx >= 1:
                    p2_tav(idx - 1)
            p2_tav(31)
            t.wait_ge(p3.ldx, 32)
            t.wait_ge(p3.ldc, 32)
            gi = 0
            for ob in range(4):
                t.wait_ge(p3.ldw[0], 16 * (ob + 1))
                for tt in range(NTT):
                    pb = p3.ps[gi % 2]
                    if gi >= 2:
                        t.wait_ge(p3.evs, gi - 1)
                    for c in range(NCH):
                        t.matmul(pb[:, :],
                                 p3.ofb[:, c * TOK + tt * P: c * TOK + (tt + 1) * P],
                                 p3.wpb[ob % 2][:, c * 512:(c + 1) * 512],
                                 start=(c == 0), stop=False)
                    t.matmul(pb[:, :], ones1[:, :],
                             p3.bprow[0:1, ob * 512:(ob + 1) * 512],
                             start=False, stop=True).then_inc(p3.mm, 1)
                    gi += 1
            ntr = 0
            for tt in range(NTT):
                t.wait_ge(p3.lnv, 7 * (tt + 1))
                for cg in range(4):
                    if ntr >= 2:
                        t.wait_ge(p3.xfs, ntr - 1)
                    last = None
                    for c4 in range(4):
                        c = cg * 4 + c4
                        last = t.transpose(p3.pxf[ntr % 2][:, c4 * P:(c4 + 1) * P],
                                           stream[:, tt * D + c * P: tt * D + (c + 1) * P],
                                           i32sb[:, :])
                    last.then_inc(p3.trs, 1)
                    ntr += 1
            for tt in range(NTT):
                t.wait_ge(p3.xfs, 4 * (tt + 1))
                if tt > 0:
                    t.wait_ge(p3.gsem, tt)
                for c in range(NCH):
                    t.matmul(p3.pg[0][:, 0:8],
                             x_f[:, c * TOK + tt * P: c * TOK + (tt + 1) * P],
                             p3.gwsb[:, c * 8:(c + 1) * 8],
                             start=(c == 0), stop=(c == NCH - 1)
                             ).then_inc(p3.mm, 1 if c == NCH - 1 else 0)
            t.wait_ge(p3.gsem, 8)
            t.wait_ge(p3.evs, 32)
            t.wait_ge(p3.xfs, 32)
            for e in range(8):
                pi = 0
                def p3_a2tr(gidx):
                    piece_ = (gidx % 16) // 8
                    fw_ = PIECES[piece_][1]
                    t.wait_ge(p3.a2s, gidx + 1)
                    if gidx >= 2:
                        t.wait_ge(p3.a2e, gidx - 1)
                    nt = fw_ // P
                    last = None
                    for c4 in range(nt):
                        last = t.transpose(p3.ptr[gidx % 2][:, c4 * P:(c4 + 1) * P],
                                           p3.a2t[gidx % 2][:, c4 * P:(c4 + 1) * P],
                                           ibfsb[:, :])
                    last.then_inc(p3.trs, 1)

                for piece, (foff, fw) in enumerate(PIECES):
                    pig = e * 2 + piece
                    par = pig % 2
                    t.wait_ge(p3.lde[0], 32 * (pig + 1))
                    for tt in range(NTT):
                        gidx = e * 16 + pi
                        if gidx >= 2:
                            t.wait_ge(p3.sil, gidx - 1)
                            t.wait_ge(p3.a2s, gidx - 1)
                        for c in range(NCH):
                            t.matmul(p3.pg[gidx % 2][:, 0:fw],
                                     x_f[:, c * TOK + tt * P: c * TOK + (tt + 1) * P],
                                     p3.w13b[2 * par][:, c * fw:(c + 1) * fw],
                                     start=(c == 0), stop=(c == NCH - 1))
                        inst = None
                        for c in range(NCH):
                            inst = t.matmul(p3.pu[gidx % 2][:, 0:fw],
                                            x_f[:, c * TOK + tt * P: c * TOK + (tt + 1) * P],
                                            p3.w13b[2 * par + 1][:, c * fw:(c + 1) * fw],
                                            start=(c == 0), stop=(c == NCH - 1))
                        inst.then_inc(p3.mm, 1)
                        if pi >= 1:
                            p3_a2tr(e * 16 + pi - 1)
                        pi += 1
                p3_a2tr(e * 16 + 15)
                for db in range(4):
                    j = e * 4 + db
                    t.wait_ge(p3.ld2[0], 16 * (j + 1))
                    for tt in range(NTT):
                        di = e * 32 + db * 8 + tt
                        if di >= 2:
                            t.wait_ge(p3.dev, di - 1)
                        t.wait_ge(p3.a2e, e * 16 + 8 + tt + 1)
                        inst = None
                        for fc in range(6):
                            inst = t.matmul(p3.ps[di % 2][:, :],
                                            p3.a2f[:, tt * 6 * P + fc * P: tt * 6 * P + (fc + 1) * P],
                                            p3.w2b[j % 2][:, fc * 512:(fc + 1) * 512],
                                            start=(fc == 0), stop=(fc == 5))
                        inst.then_inc(p3.dmm, 6)
            t.wait_ge(p3.dev, 256)
            t.wait_ge(p3.mm, 168)
            ntr = 0
            for tt in range(NTT):
                for cg in range(4):
                    if cg == 0:
                        t.wait_ge(p3.lnv, 56 + 7 * (tt + 1))
                    if ntr >= 2:
                        t.wait_ge(p4.xfs, ntr - 1)
                    last = None
                    for c4 in range(4):
                        c = cg * 4 + c4
                        last = t.transpose(p4.ps[ntr % 2][:, c4 * P:(c4 + 1) * P],
                                           stream[:, tt * D + c * P: tt * D + (c + 1) * P],
                                           i32sb[:, :])
                    last.then_inc(p4.trs, 1)
                    ntr += 1
            t.wait_ge(p4.ldc, 48)
            t.wait_ge(p4.xfs, 32)
            for tt in range(NTT):
                if tt > 0:
                    t.wait_ge(p4.gsig, tt)
                inst = None
                for c in range(NCH):
                    inst = t.matmul(p4.pgk[:, :],
                                    x_f[:, c * TOK + tt * P: c * TOK + (tt + 1) * P],
                                    p4.wgk[:, c * 32:(c + 1) * 32],
                                    start=(c == 0), stop=(c == NCH - 1))
                inst.then_inc(p4.mgk, 1)
            t.wait_ge(p4.ldc, 48)
            gi = 0
            for mi in range(3):
                for ob in range(4):
                    t.wait_ge(p4.ldw[0], 16 * (gi + 1))
                    for tt in range(NTT):
                        grp = gi * 8 + tt
                        pb = p4.ps[grp % 2]
                        if grp >= 2:
                            t.wait_ge(p4.esem, grp - 1)
                        inst = None
                        for c in range(NCH):
                            inst = t.matmul(pb[:, :],
                                            x_f[:, c * TOK + tt * P: c * TOK + (tt + 1) * P],
                                            p4.wblk[gi % 2][:, c * 512:(c + 1) * 512],
                                            start=(c == 0), stop=(c == NCH - 1))
                        inst.then_inc(p4.mm, 1)
                    gi += 1
            t.wait_ge(p4.esem, 96)
            t.wait_ge(p4.gsig, 16)
            t.wait_ge(p5.gqs, 16)
            for kt in range(16):
                if kt >= 1:
                    t.wait_ge(p5.gtr, 2 * kt)
                t.transpose(p5.pbc[0:32, 0:128], p5.gsb[:, kt * 32:(kt + 1) * 32],
                            i32sb[:, :]).then_inc(p5.gtr, 1)
            t.wait_ge(p5.gtr, 32)
            for pr in range(16):
                buf5 = (pr // 2) % 2
                sub5 = (pr % 2) * 64
                t.wait_ge(p5.qld, 48 * (pr // 2 + 1))
                t.wait_ge(p5.gld[0], 16 * (pr + 1))
                if pr >= 1:
                    t.wait_ge(p5.esem, p5.sched[(pr - 1) * 4 + 3]["exp1"])
                    t.wait_ge(p5.bce, 4 * pr)
                last = None
                for j in range(16):
                    last = t.transpose(p5.pbc[:, j:j + 1],
                                       p5.grow[pr % 2][0:1, j * P:(j + 1) * P],
                                       i32sb[0:1, 0:1])
                last.then_inc(p5.gqt, 1)
                t.wait_ge(p5.gqe, pr + 1)
                for j in range(4):
                    if pr > 0 or j > 0:
                        t.wait_ge(p5.bce, 4 * pr + j)
                    t.matmul(p5.pbc[:, :], onec32[:, :],
                             p5.grow[pr % 2][0:1, j * 512:(j + 1) * 512],
                             start=True, stop=True).then_inc(p5.bcs, 1)
                t.wait_ge(p5.bce, 4 * (pr + 1))

                def p5_smms(idx):
                    gq = idx % 4
                    rec = p5.sched[idx]
                    smm = rec["smm0"]
                    for i in range(4):
                        qt = 4 * gq + i
                        for (kc, cw) in chunks_of(qt):
                            if smm >= 4:
                                t.wait_ge(p5.asem, smm - 3)
                            t.matmul(p5.ps_s[smm % 4][:, 0:cw],
                                     p5.qsb[buf5][sub5:sub5 + 64, qt * P:(qt + 1) * P],
                                     p5.ksb[buf5][sub5:sub5 + 64, kc * 512:kc * 512 + cw],
                                     start=True, stop=True).then_inc(p5.ssem, 1)
                            smm += 1

                def p5_tav(idx):
                    gq = idx % 4
                    rec = p5.sched[idx]
                    t.wait_ge(p5.asem, rec["smm1"])
                    nkt = 4 * gq + 4

                    def p5_tr(kt):
                        ktc = rec["kt0"] + kt
                        i0 = max(0, kt - 4 * gq)
                        if ktc >= 2:
                            sem_ = p5.ptsem if ktc % 2 == 0 else p5.ptsem_s
                            t.wait_ge(sem_, (ktc - 2) // 2 + 1)
                        last = None
                        for i in range(i0, 4):
                            last = t.transpose(p5.pt_ps[ktc % 2][:, i * P:(i + 1) * P],
                                               p5.asb[idx % 2][:, i * T + kt * P: i * T + (kt + 1) * P],
                                               ibfsb[:, :])
                        last.then_inc(p5.tsem, 1)

                    def p5_av(kt):
                        ktc = rec["kt0"] + kt
                        i0 = max(0, kt - 4 * gq)
                        sem_ = p5.ptsem if ktc % 2 == 0 else p5.ptsem_s
                        t.wait_ge(sem_, ktc // 2 + 1)
                        if kt == 0 and rec["oev0"] > 0:
                            t.wait_ge(p5.osem, rec["oev0"])
                        t.matmul(p5.o_ps[0:64, i0 * P:512],
                                 p5.vsb[buf5][:, kt * P + sub5: kt * P + sub5 + 64],
                                 p5.ptsb[ktc % 2][:, i0 * P:512],
                                 start=(kt == 0), stop=(kt == nkt - 1)
                                 ).then_inc(p5.avsem, 1)

                    p5_tr(0)
                    for kt in range(1, nkt):
                        p5_tr(kt)
                        p5_av(kt - 1)
                    p5_av(nkt - 1)

                for gq in range(4):
                    p5_smms(pr * 4 + gq)
                    if gq >= 1:
                        p5_tav(pr * 4 + gq - 1)
                p5_tav(pr * 4 + 3)
            t.wait_ge(p5.osem, 64)
            gi = 0
            for ob in range(8):
                t.wait_ge(p6.ldw[0], 16 * (ob + 1))
                for tt in range(NTT):
                    pb = p6.ps[gi % 2]
                    if gi >= 2:
                        t.wait_ge(p6.sil, gi - 1)
                    inst = None
                    for c in range(NCH):
                        inst = t.matmul(pb[:, 0:256],
                                        x_f[:, c * TOK + tt * P: c * TOK + (tt + 1) * P],
                                        p6.wb[ob % 2][:, c * 256:(c + 1) * 256],
                                        start=(c == 0), stop=(c == NCH - 1))
                    inst.then_inc(p6.mm, 1)
                    gi += 1
            ntr = 0
            for tt in range(NTT):
                t.wait_ge(p6.onr, 5 * tt + 5)
                for cg in range(4):
                    if ntr >= 2:
                        t.wait_ge(p6.xfs, ntr - 1)
                    last = None
                    for c4 in range(4):
                        c = cg * 4 + c4
                        last = t.transpose(p6.ptrb[ntr % 2][:, c4 * P:(c4 + 1) * P],
                                           p6.sg_sb[:, tt * D + c * P: tt * D + (c + 1) * P],
                                           ibfsb[:, :])
                    last.then_inc(p6.trs, 1)
                    ntr += 1
            gi = 0
            for ob in range(8):
                bi = 8 + ob
                t.wait_ge(p6.ldw[0], 16 * (bi + 1))
                for tt in range(NTT):
                    grp = NGP6 + gi
                    pb = p6.ps[grp % 2]
                    if gi >= 2:
                        t.wait_ge(p6.evs, gi - 1)
                    else:
                        t.wait_ge(p6.sil, NGP6)
                    t.wait_ge(p6.xfs, 4 * (tt + 1))
                    inst = None
                    for c in range(NCH):
                        inst = t.matmul(pb[:, 0:256],
                                        p6.ogf[:, c * TOK + tt * P: c * TOK + (tt + 1) * P],
                                        p6.wb[bi % 2][:, c * 256:(c + 1) * 256],
                                        start=(c == 0), stop=(c == NCH - 1))
                    inst.then_inc(p6.mm, 1)
                    gi += 1
            ntr = 32
            for tt in range(NTT):
                t.wait_ge(p6.lnv, 7 * (tt + 1))
                for cg in range(4):
                    if ntr >= 34:
                        t.wait_ge(p6.xfs, ntr - 1)
                    else:
                        t.wait_ge(p6.evs, NWO6)
                    last = None
                    for c4 in range(4):
                        c = cg * 4 + c4
                        last = t.transpose(p6.ps[ntr % 2][:, c4 * P:(c4 + 1) * P],
                                           stream[:, tt * D + c * P: tt * D + (c + 1) * P],
                                           i32sb[:, :])
                    last.then_inc(p6.trs, 1)
                    ntr += 1
            for th in range(2):
                tsl = slice(th * 512, (th + 1) * 512)
                t.wait_ge(p6.xfs, 64)
                if th == 1:
                    t.wait_ge(p6.rse, 2)
                inst = None
                for c in range(NCH):
                    inst = t.matmul(p6.prow[0:1, :], nones[:, :],
                                    x_f[:, c * TOK + th * 512: c * TOK + (th + 1) * 512],
                                    start=(c == 0), stop=(c == NCH - 1))
                inst.then_inc(p6.rsm, 1)
                t.wait_ge(p6.rse, 2 * th + 1)
                if th == 1:
                    t.wait_ge(p6.dse, NCH)
                for hc in range(NHC):
                    qi = th * NHC + hc
                    t.wait_ge(p6.ldq[0], 32 * (qi + 1))
                    if qi >= 2:
                        t.wait_ge(p6.gup, 2 * (qi - 1))
                    for c in range(NCH):
                        t.matmul(p6.ps[qi % 2][:, :],
                                 p6.wqg_b[qi % 2][:, c * P:(c + 1) * P],
                                 x_f[:, c * TOK + th * 512: c * TOK + (th + 1) * 512],
                                 start=(c == 0), stop=False)
                    t.matmul(p6.ps[qi % 2][:, :], ones1[:, :], p6.rsrow[0:1, tsl],
                             start=False, stop=True)
                    inst = None
                    for c in range(NCH):
                        t.matmul(p6.pu[qi % 2][:, :],
                                 p6.wqu_b[qi % 2][:, c * P:(c + 1) * P],
                                 x_f[:, c * TOK + th * 512: c * TOK + (th + 1) * 512],
                                 start=(c == 0), stop=False)
                    inst = t.matmul(p6.pu[qi % 2][:, :], ones1[:, :], p6.rsrow[0:1, tsl],
                                    start=False, stop=True)
                    inst.then_inc(p6.mm, 1)
                t.wait_ge(p6.a2s, NHC * (th + 1))
                inst = None
                for hc in range(NHC):
                    inst = t.matmul(p6.prow[0:1, :], nones[:, :],
                                    p6.a2sb[:, hc * 512:(hc + 1) * 512],
                                    start=(hc == 0), stop=(hc == NHC - 1))
                inst.then_inc(p6.rsm, 1)
                t.wait_ge(p6.rse, 2 * th + 2)
                def p6_dtr(pdi):
                    t.wait_ge(p6.dse, pdi + 1)
                    if pdi >= 2:
                        t.wait_ge(p6.dev, pdi - 1)
                    last = None
                    for j in range(4):
                        last = t.transpose(p6.ptrb[pdi % 2][:, j * P:(j + 1) * P],
                                           p6.yfb[pdi % 2][:, j * P:(j + 1) * P],
                                           ibfsb[:, :])
                    last.then_inc(p6.dtr, 1)

                for dc in range(NCH):
                    di = th * NCH + dc
                    t.wait_ge(p6.ldd[0], 16 * (di + 1))
                    if di >= 2:
                        t.wait_ge(p6.dse, di - 1)
                    for hc in range(NHC):
                        t.matmul(p6.pu[di % 2][:, :],
                                 p6.wqd_b[di % 2][:, hc * P:(hc + 1) * P],
                                 p6.a2sb[:, hc * 512:(hc + 1) * 512],
                                 start=(hc == 0), stop=False)
                    inst = t.matmul(p6.pu[di % 2][:, :], ones1[:, :], p6.csrow[0:1, tsl],
                                    start=False, stop=True)
                    inst.then_inc(p6.dmm, 1)
                    if dc >= 1:
                        p6_dtr(di - 1)
                    if dc == NCH - 1:
                        p6_dtr(di)

        @block.vector
        def _(v):
            v.wait_ge(cst, 81)
            gi = 0
            for ob in range(12):
                for tt in range(NTT):
                    v.wait_ge(p1.mm, gi + 1)
                    if ob >= 2:
                        v.wait_ge(p1.so[0], 16 * (ob - 1))
                    v.tensor_copy(p1.ev[ob % 2][:, tt * 512:(tt + 1) * 512],
                                  p1.ps[gi % 2][:, :]).then_inc(p1.evs, 1)
                    gi += 1
            nvself = [0]
            v.wait_ge(p1.so[0], 16 * 12)

            def p2_v_softmax(idx):
                h, gq = divmod(idx, 4)
                rec = p2.sched[idx]
                smm = rec["smm0"]
                for i in range(4):
                    qt = 4 * gq + i
                    chs = chunks_of(qt)
                    for (kc, cw) in chs:
                        if kc == len(chs) - 1:
                            v.wait_ge(p2.ssem, smm + 1)
                            v.tensor_tensor(p2.ps_s[smm % 4][:, cw - P:cw],
                                            p2.ps_s[smm % 4][:, cw - P:cw],
                                            mnegsb[:, :], ALU.add).then_inc(p2.msem, 1)
                        smm += 1
                    v.wait_ge(p2.esem, smm)
                    nkc = len(chs)
                    zq = rec["zsc0"] + i
                    if zq > 0:
                        v.wait_ge(p2.zsem, zq)
                    v.reduce_sum(p2.zinv[:, i:i + 1], p2.zp[:, i * 4:i * 4 + nkc],
                                 AX.X).then_inc(p2.vself, 1)
                    nvself[0] += 1
                    v.wait_ge(p2.vself, nvself[0])
                    v.reciprocal(p2.zinv[:, i:i + 1], p2.zinv[:, i:i + 1]
                                 ).then_inc(p2.vself, 1)
                    nvself[0] += 1
                    v.wait_ge(p2.vself, nvself[0])
                    v.tensor_scalar_mul(p2.psb[idx % 2][:, i * T: i * T + (qt + 1) * P],
                                        p2.psb[idx % 2][:, i * T: i * T + (qt + 1) * P],
                                        p2.zinv[:, i:i + 1]).then_inc(p2.zsem, 1)

            def p2_v_evacs(idx):
                h, gq = divmod(idx, 4)
                rec = p2.sched[idx]
                ktc = rec["kt0"]
                for kt in range(4 * gq + 4):
                    i0 = max(0, kt - 4 * gq)
                    v.wait_ge(p2.tsem, ktc + 1)
                    v.tensor_copy(p2.ptsb[ktc % 2][:, i0 * P:512],
                                  p2.pt_ps[ktc % 2][:, i0 * P:512]).then_inc(p2.ptsem, 1)
                    ktc += 1
                v.wait_ge(p2.avsem, rec["kt1"])
                if h >= 2 and gq == 0:
                    v.wait_ge(p2.sto[0], 16 * (h - 1))
                v.tensor_copy(p2.osb[h % 2][:, gq * 512:(gq + 1) * 512],
                              p2.o_ps[:, :]).then_inc(p2.osem, 1)

            for idx in range(32):
                p2_v_softmax(idx)
                if idx >= 1:
                    p2_v_evacs(idx - 1)
            p2_v_evacs(31)
            gi = 0
            for ob in range(4):
                for tt in range(NTT):
                    v.wait_ge(p3.mm, gi + 1)
                    v.tensor_tensor(stream[:, tt * D + ob * 512: tt * D + (ob + 1) * 512],
                                    p3.ps[gi % 2][:, :],
                                    stream[:, tt * D + ob * 512: tt * D + (ob + 1) * 512],
                                    ALU.add).then_inc(p3.evs, 1)
                    gi += 1

            def ln_vector(ph, tt, lnv_base, lns_base):
                sl = slice(tt * D, (tt + 1) * D)
                v.reduce_sum(sums[:, tt:tt + 1], stream[:, sl], AX.X).then_inc(ph.lnv, 1)
                v.wait_ge(ph.lns, lns_base + 2 * tt + 1)
                v.wait_ge(ph.lnv, lnv_base + 7 * tt + 1)
                v.tensor_scalar_mul(mu[:, tt:tt + 1], sums[:, tt:tt + 1], 1.0 / D
                                    ).then_inc(ph.lnv, 1)
                v.wait_ge(ph.lnv, lnv_base + 7 * tt + 2)
                v.tensor_scalar_mul(varv[:, tt:tt + 1], sq[:, tt:tt + 1], 1.0 / D
                                    ).then_inc(ph.lnv, 1)
                v.wait_ge(ph.lnv, lnv_base + 7 * tt + 3)
                v.scalar_tensor_tensor(varv[:, tt:tt + 1], mu[:, tt:tt + 1], mu[:, tt:tt + 1],
                                       varv[:, tt:tt + 1], ALU.mult, ALU.subtract
                                       ).then_inc(ph.lnv, 1)
                v.wait_ge(ph.lns, lns_base + 2 * tt + 2)
                v.reciprocal(rsd[:, tt:tt + 1], rsd[:, tt:tt + 1]).then_inc(ph.lnv, 1)
                v.wait_ge(ph.lnv, lnv_base + 7 * tt + 5)
                v.tensor_scalar(stream[:, sl], stream[:, sl],
                                mu[:, tt:tt + 1], rsd[:, tt:tt + 1],
                                ALU.subtract, ALU.mult).then_inc(ph.lnv, 2)

            v.wait_ge(p3.evs, 32)
            for tt in range(NTT):
                ln_vector(p3, tt, 0, 0)
            NGOP = 9
            for tt in range(NTT):
                g8 = slice(tt * 8, (tt + 1) * 8)
                v.wait_ge(p3.gsem, tt + 1)
                if tt > 0:
                    v.wait_ge(p3.vch, NGOP * tt)
                v.reduce_max(p3.t2a[:, :], p3.gexp[:, g8], AX.X).then_inc(p3.vch, 1)
                v.wait_ge(p3.vch, NGOP * tt + 1)
                v.tensor_scalar(p3.e8a[:, :], p3.gexp[:, g8], p3.t2a[:, 0:1], None,
                                ALU.is_equal).then_inc(p3.vch, 1)
                v.wait_ge(p3.vch, NGOP * tt + 2)
                v.scalar_tensor_tensor(p3.e8a[:, :], p3.e8a[:, :], -1e30, p3.gexp[:, g8],
                                       ALU.mult, ALU.add).then_inc(p3.vch, 1)
                v.wait_ge(p3.vch, NGOP * tt + 3)
                v.reduce_max(p3.t2b[:, :], p3.e8a[:, :], AX.X).then_inc(p3.vch, 1)
                v.wait_ge(p3.vch, NGOP * tt + 4)
                v.tensor_scalar(p3.e8a[:, :], p3.gexp[:, g8], p3.t2b[:, 0:1], None,
                                ALU.is_ge).then_inc(p3.vch, 1)
                v.wait_ge(p3.vch, NGOP * tt + 5)
                v.tensor_tensor(p3.e8a[:, :], p3.e8a[:, :], p3.gexp[:, g8],
                                ALU.mult).then_inc(p3.vch, 1)
                v.tensor_tensor(p3.t2a[:, :], p3.t2a[:, :], p3.t2b[:, :],
                                ALU.add).then_inc(p3.vch, 1)
                v.wait_ge(p3.vch, NGOP * tt + 7)
                v.reciprocal(p3.t2a[:, :], p3.t2a[:, :]).then_inc(p3.vch, 1)
                v.wait_ge(p3.vch, NGOP * tt + 8)
                v.tensor_scalar_mul(p3.comb[:, g8], p3.e8a[:, :], p3.t2a[:, 0:1]
                                    ).then_inc(p3.vch, 1)
            v.wait_ge(p3.vch, NGOP * NTT)
            for e in range(8):
                pi = 0
                for (foff, fw) in PIECES:
                    for tt in range(NTT):
                        gidx = e * 16 + pi
                        v.wait_ge(p3.sil, gidx + 1)
                        if gidx >= 2:
                            v.wait_ge(p3.trs, 32 + gidx - 1)
                        v.scalar_tensor_tensor(p3.a2t[gidx % 2][:, 0:fw],
                                               p3.pu[gidx % 2][:, 0:fw],
                                               p3.comb[:, tt * 8 + e: tt * 8 + e + 1],
                                               p3.gs[gidx % 2][:, 0:fw],
                                               ALU.mult, ALU.mult).then_inc(p3.a2s, 1)
                        pi += 1
                for db in range(4):
                    for tt in range(NTT):
                        di = e * 32 + db * 8 + tt
                        v.wait_ge(p3.dmm, 6 * (di + 1))
                        v.tensor_tensor(stream[:, tt * D + db * 512: tt * D + (db + 1) * 512],
                                        p3.ps[di % 2][:, :],
                                        stream[:, tt * D + db * 512: tt * D + (db + 1) * 512],
                                        ALU.add).then_inc(p3.dev, 1)
            v.wait_ge(p3.dev, 256)
            for tt in range(NTT):
                ln_vector(p3, tt, 56, 16)
            nv4 = [0]
            v.wait_ge(p4.ldc, 48)
            for mi in range(2):
                for tt in range(NTT):
                    v.wait_ge(p4.esem, mi * 32 + 24 + tt + 1)
                    v.wait_ge(p4.vch, nv4[0])
                    st3 = p4.stage[:, tt * D:(tt + 1) * D].rearrange("p (h f) -> p h f", h=32)
                    x1 = st3[:, :, 0:32]
                    x2v = st3[:, :, 32:64]
                    ccb = p4.cosb[:, tt * 64: tt * 64 + 32].unsqueeze(1).broadcast_to([P, 32, 32])
                    ssb = p4.sinb[:, tt * 64: tt * 64 + 32].unsqueeze(1).broadcast_to([P, 32, 32])
                    t3 = [p4.tmp[i][:, :].rearrange("p (h f) -> p h f", h=32) for i in range(4)]
                    v.tensor_tensor(t3[0], x1, ccb, ALU.mult).then_inc(p4.vch, 1); nv4[0] += 1
                    v.tensor_tensor(t3[1], x2v, ssb, ALU.mult).then_inc(p4.vch, 1); nv4[0] += 1
                    v.tensor_tensor(t3[2], x1, ssb, ALU.mult).then_inc(p4.vch, 1); nv4[0] += 1
                    v.tensor_tensor(t3[3], x2v, ccb, ALU.mult).then_inc(p4.vch, 1); nv4[0] += 1
                    v.wait_ge(p4.vch, nv4[0])
                    v.tensor_tensor(x1, t3[0], t3[1], ALU.subtract).then_inc(p4.vch, 1); nv4[0] += 1
                    v.tensor_tensor(x2v, t3[2], t3[3], ALU.add).then_inc(p4.vch, 1); nv4[0] += 1
            v.wait_ge(p5.gtr, 32)
            v.tensor_tensor_scan(p5.gcum[:, :], p5.gf[:, :], p5.gf[:, :], 0.0,
                                 ALU.add, ALU.bypass).then_inc(p5.gsc, 1)
            nv5 = [0]
            v.wait_ge(p5.gst, 16)

            def p5_v_amul(idx):
                gq = idx % 4
                rec = p5.sched[idx]
                smm = rec["smm0"]
                if idx >= 2:
                    v.wait_ge(p5.tsem, p5.sched[idx - 2]["kt1"])
                for i in range(4):
                    qt = 4 * gq + i
                    chs = chunks_of(qt)
                    for (kc, cw) in chs:
                        isdiag = (kc == len(chs) - 1)
                        v.wait_ge(p5.esem, smm + 1)
                        if isdiag:
                            v.tensor_tensor(p5.esb[smm % 2][:, cw - P:cw],
                                            p5.esb[smm % 2][:, cw - P:cw],
                                            mbfsb[:, :], ALU.mult).then_inc(p5.vsf, 1)
                            nv5[0] += 1
                            v.wait_ge(p5.vsf, nv5[0])
                        v.wait_ge(p5.ssem, smm + 1)
                        v.scalar_tensor_tensor(
                            p5.asb[idx % 2][:, i * T + kc * 512: i * T + kc * 512 + cw],
                            p5.ps_s[smm % 4][:, 0:cw], 0.125,
                            p5.esb[smm % 2][:, 0:cw],
                            ALU.mult, ALU.mult).then_inc(p5.asem, 1)
                        smm += 1

            def p5_v_evacs(idx):
                pr, gq = divmod(idx, 4)
                rec = p5.sched[idx]
                ktc = rec["kt0"]
                for kt in range(4 * gq + 4):
                    i0 = max(0, kt - 4 * gq)
                    if ktc % 2 == 0:
                        v.wait_ge(p5.tsem, ktc + 1)
                        v.tensor_copy(p5.ptsb[ktc % 2][:, i0 * P:512],
                                      p5.pt_ps[ktc % 2][:, i0 * P:512]).then_inc(p5.ptsem, 1)
                    ktc += 1
                v.wait_ge(p5.avsem, rec["kt1"])
                if pr >= 2 and gq == 0:
                    v.wait_ge(p5.sto[0], 16 * (pr - 1))
                v.tensor_copy(p5.osb[pr % 2][:, gq * 512:(gq + 1) * 512],
                              p5.o_ps[0:64, :]).then_inc(p5.osem, 1)

            for idx5 in range(64):
                if idx5 >= 1 and idx5 % 4 == 0:
                    p5_v_evacs(idx5 - 1)
                p5_v_amul(idx5)
                if idx5 >= 1 and idx5 % 4 != 0:
                    p5_v_evacs(idx5 - 1)
            p5_v_evacs(63)

            v.wait_ge(p3.lnv, 112)
            for tt in range(NTT):
                v.wait_ge(p6.ldo[0], 16 * (tt + 1))
                v.wait_ge(p6.onr, 5 * tt)
                v.wait_ge(p6.lns, 2 * tt + 1)
                o3 = p6.otb[tt % 2][:, :].rearrange("p (h f) -> p h f", h=32)
                v.reduce_sum(p6.hvar[:, :].unsqueeze(2),
                             scr[:, :].rearrange("p (h f) -> p h f", h=32),
                             AX.X).then_inc(p6.onr, 1)
                v.wait_ge(p6.onr, 5 * tt + 1)
                v.wait_ge(p6.lns, 2 * tt + 2)
                v.reciprocal(p6.hvar[:, :], p6.hvar[:, :]).then_inc(p6.onr, 1)
                v.wait_ge(p6.onr, 5 * tt + 2)
                v.tensor_tensor(p6.otb[tt % 2][:, :].rearrange("p (h f) -> p h f", h=32),
                                o3, p6.hvar[:, :].unsqueeze(2).broadcast_to([P, 32, 64]),
                                ALU.mult).then_inc(p6.onr, 1)
                v.wait_ge(p6.onr, 5 * tt + 3)
                v.wait_ge(p6.sil, NGP6)
                v.tensor_tensor(p6.sg_sb[:, tt * D:(tt + 1) * D],
                                p6.sg_sb[:, tt * D:(tt + 1) * D],
                                p6.otb[tt % 2][:, :], ALU.mult).then_inc(p6.onr, 2)
            gi = 0
            for ob in range(8):
                for tt in range(NTT):
                    grp = NGP6 + gi
                    v.wait_ge(p6.mm, grp + 1)
                    v.tensor_tensor(stream[:, tt * D + ob * 256: tt * D + (ob + 1) * 256],
                                    p6.ps[grp % 2][:, 0:256],
                                    stream[:, tt * D + ob * 256: tt * D + (ob + 1) * 256],
                                    ALU.add).then_inc(p6.evs, 1)
                    gi += 1
            v.wait_ge(p6.evs, NWO6)
            for tt in range(NTT):
                ln_vector(p6, tt, 0, 16)
            for th in range(2):
                for hc in range(NHC):
                    qi = th * NHC + hc
                    v.wait_ge(p6.gup, 2 * (qi + 1))
                    v.tensor_tensor(p6.a2sb[:, hc * 512:(hc + 1) * 512],
                                    p6.sgh[qi % 2][:, :],
                                    p6.a2sb[:, hc * 512:(hc + 1) * 512],
                                    ALU.mult).then_inc(p6.a2s, 1)
                for dc in range(NCH):
                    di = th * NCH + dc
                    v.wait_ge(p6.dtr, di + 1)
                    for j in range(4):
                        tt = th * 4 + j
                        inst = v.tensor_tensor(
                            stream[:, tt * D + dc * P: tt * D + (dc + 1) * P],
                            p6.ptrb[di % 2][:, j * P:(j + 1) * P],
                            stream[:, tt * D + dc * P: tt * D + (dc + 1) * P],
                            ALU.add)
                        if j == 3:
                            inst.then_inc(p6.dev, 1)
            v.wait_ge(p6.dev, 32)
            for tt in range(NTT):
                ln_vector(p6, tt, 56, 32)

        @block.scalar
        def _(s):
            s.wait_ge(cst, 81)
            SCALE2 = 128.0 ** -0.5
            s.wait_ge(p1.so[0], 16 * 12)
            for h in range(8):
                for gq in range(4):
                    idx = h * 4 + gq
                    rec = p2.sched[idx]
                    if idx > 1:
                        prev = p2.sched[idx - 2]
                        s.wait_ge(p2.tsem, prev["kt1"])
                    if idx > 0:
                        s.wait_ge(p2.zsem, p2.sched[idx - 1]["zsc1"])
                    smm = rec["smm0"]
                    mk = rec["mask0"]
                    for i in range(4):
                        qt = 4 * gq + i
                        chs = chunks_of(qt)
                        for (kc, cw) in chs:
                            if kc == len(chs) - 1:
                                mk += 1
                                s.wait_ge(p2.msem, mk)
                            else:
                                s.wait_ge(p2.ssem, smm + 1)
                            s.activation(p2.psb[idx % 2][:, i * T + kc * 512: i * T + kc * 512 + cw],
                                         p2.ps_s[smm % 4][:, 0:cw], AF.Exp,
                                         bias=shcol[:, :], scale=SCALE2,
                                         accum_out=p2.zp[:, i * 4 + kc:i * 4 + kc + 1]
                                         ).then_inc(p2.esem, 1)
                            smm += 1
            def ln_scalar(ph, tt, lnv_base, lns_base):
                sl = slice(tt * D, (tt + 1) * D)
                s.wait_ge(ph.lns, lns_base + 2 * tt)
                s.activation(scr[:, :], stream[:, sl], AF.Square,
                             accum_out=sq[:, tt:tt + 1]).then_inc(ph.lns, 1)
                s.wait_ge(ph.lnv, lnv_base + 7 * tt + 4)
                s.activation(rsd[:, tt:tt + 1], varv[:, tt:tt + 1], AF.Sqrt,
                             bias=epsln[:, :], scale=-1.0).then_inc(ph.lns, 1)

            s.wait_ge(p3.evs, 32)
            for tt in range(NTT):
                ln_scalar(p3, tt, 0, 0)
            ntr = 0
            for tt in range(NTT):
                for cg in range(4):
                    s.wait_ge(p3.trs, ntr + 1)
                    s.copy(x_f[:, :].rearrange("p (c t) -> p c t", c=NCH)
                           [:, cg * 4:(cg + 1) * 4, tt * P:(tt + 1) * P],
                           p3.pxf[ntr % 2][:, :].rearrange("p (c4 t) -> p c4 t", c4=4)
                           ).then_inc(p3.xfs, 1)
                    ntr += 1
            for tt in range(NTT):
                s.wait_ge(p3.mm, 32 + tt + 1)
                s.activation(p3.gexp[:, tt * 8:(tt + 1) * 8], p3.pg[0][:, 0:8],
                             AF.Exp).then_inc(p3.gsem, 1)
            for e in range(8):
                pi = 0
                for (foff, fw) in PIECES:
                    for tt in range(NTT):
                        gidx = e * 16 + pi
                        s.wait_ge(p3.mm, mm_gu3(gidx))
                        if gidx >= 2:
                            s.wait_ge(p3.a2s, gidx - 1)
                        s.activation(p3.gs[gidx % 2][:, 0:fw], p3.pg[gidx % 2][:, 0:fw],
                                     AF.Silu).then_inc(p3.sil, 1)
                        s.wait_ge(p3.trs, 32 + gidx + 1)
                        if e >= 1 and pi == 0:
                            s.wait_ge(p3.dmm, 192 * e)
                        nt = fw // P
                        s.copy(p3.a2f[:, tt * 6 * P + (foff // P) * P:
                                      tt * 6 * P + (foff // P) * P + nt * P],
                               p3.ptr[gidx % 2][:, 0:nt * P]).then_inc(p3.a2e, 1)
                        pi += 1
            s.wait_ge(p3.dev, 256)
            for tt in range(NTT):
                ln_scalar(p3, tt, 56, 16)
            ntr = 0
            for tt in range(NTT):
                for cg in range(4):
                    s.wait_ge(p4.trs, ntr + 1)
                    s.copy(x_f[:, :].rearrange("p (c t) -> p c t", c=NCH)
                           [:, cg * 4:(cg + 1) * 4, tt * P:(tt + 1) * P],
                           p4.ps[ntr % 2][:, :].rearrange("p (c4 t) -> p c4 t", c4=4)
                           ).then_inc(p4.xfs, 1)
                    ntr += 1
            for tt in range(NTT):
                s.wait_ge(p4.mgk, tt + 1)
                s.activation(p4.gks[:, tt * 32:(tt + 1) * 32], p4.pgk[:, :],
                             AF.Sigmoid).then_inc(p4.gsig, 1)
            for tt in range(NTT):
                s.wait_ge(p4.gsig, NTT + tt)
                s.activation(p4.gks[:, tt * 32:(tt + 1) * 32],
                             p4.gks[:, tt * 32:(tt + 1) * 32],
                             AF.Ln).then_inc(p4.gsig, 1)
            gi = 0
            for mi in range(3):
                for ob in range(4):
                    for tt in range(NTT):
                        grp = gi * 8 + tt
                        s.wait_ge(p4.mm, grp + 1)
                        if ob == 0 and tt == 0:
                            s.wait_ge(p4.sto, 16 * (mi + 1))
                        s.copy(p4.stage[:, tt * D + ob * 512: tt * D + (ob + 1) * 512],
                               p4.ps[grp % 2][:, :]).then_inc(p4.esem, 1)
                    gi += 1
            for kt in range(16):
                s.wait_ge(p5.gtr, 2 * kt + 1)
                s.copy(p5.gf[0:32, kt * P:(kt + 1) * P],
                       p5.pbc[0:32, 0:P]).then_inc(p5.gtr, 1)
            def p5_s_evac(pidx):
                prec = p5.sched[pidx]
                pgq = pidx % 4
                pktc = prec["kt0"]
                for pkt in range(4 * pgq + 4):
                    pi0 = max(0, pkt - 4 * pgq)
                    if pktc % 2 == 1:
                        s.wait_ge(p5.tsem, pktc + 1)
                        s.copy(p5.ptsb[pktc % 2][:, pi0 * P:512],
                               p5.pt_ps[pktc % 2][:, pi0 * P:512]
                               ).then_inc(p5.ptsem_s, 1)
                    pktc += 1

            for pr in range(16):
                if pr >= 1:
                    p5_s_evac(pr * 4 - 1)
                s.wait_ge(p5.gqt, pr + 1)
                if pr >= 1:
                    s.wait_ge(p5.esem, p5.sched[(pr - 1) * 4 + 3]["exp1"])
                s.activation(p5.gq16[:, :], p5.pbc[:, 0:16], AF.Copy,
                             scale=1.0 / 16.0).then_inc(p5.gqe, 1)
                for j in range(4):
                    s.wait_ge(p5.bcs, 4 * pr + j + 1)
                    s.copy(p5.gkbc[:, j * 512:(j + 1) * 512], p5.pbc[:, :]
                           ).then_inc(p5.bce, 1)
                for gq in range(4):
                    rec = p5.sched[pr * 4 + gq]
                    smm = rec["smm0"]
                    if gq == 0:
                        s.wait_ge(p5.bce, 4 * (pr + 1))
                    for i in range(4):
                        qt = 4 * gq + i
                        for (kc, cw) in chunks_of(qt):
                            if smm >= 2:
                                s.wait_ge(p5.asem, smm - 1)
                            s.activation(p5.esb[smm % 2][:, 0:cw],
                                         p5.gkbc[:, kc * 512:kc * 512 + cw],
                                         AF.Exp, bias=p5.gq16[:, qt:qt + 1],
                                         scale=-1.0 / 16.0).then_inc(p5.esem, 1)
                            smm += 1
                    if gq >= 1:
                        p5_s_evac(pr * 4 + gq - 1)
            p5_s_evac(63)
            s.wait_ge(p3.lns, 32)
            s.wait_ge(p6.ldm, 48)
            gi = 0
            for ob in range(8):
                for tt in range(NTT):
                    s.wait_ge(p6.mm, gi + 1)
                    s.activation(p6.sg_sb[:, tt * D + ob * 256: tt * D + (ob + 1) * 256],
                                 p6.ps[gi % 2][:, 0:256], AF.Silu).then_inc(p6.sil, 1)
                    gi += 1
            for tt in range(NTT):
                s.wait_ge(p6.ldo[0], 16 * (tt + 1))
                if tt >= 1:
                    s.wait_ge(p6.onr, 5 * (tt - 1) + 1)
                s.activation(scr[:, :], p6.otb[tt % 2][:, :], AF.Square).then_inc(p6.lns, 1)
                s.wait_ge(p6.onr, 5 * tt + 1)
                s.activation(p6.hvar[:, :], p6.hvar[:, :], AF.Sqrt,
                             bias=epsrm[:, :], scale=1.0 / 64.0).then_inc(p6.lns, 1)
            ntr = 0
            for tt in range(NTT):
                for cg in range(4):
                    s.wait_ge(p6.trs, ntr + 1)
                    s.copy(p6.ogf[:, :].rearrange("p (c t) -> p c t", c=NCH)
                           [:, cg * 4:(cg + 1) * 4, tt * P:(tt + 1) * P],
                           p6.ptrb[ntr % 2][:, 0:512].rearrange("p (c4 t) -> p c4 t", c4=4)
                           ).then_inc(p6.xfs, 1)
                    ntr += 1
            s.wait_ge(p6.evs, NWO6)
            for tt in range(NTT):
                ln_scalar(p6, tt, 0, 16)
            for tt in range(NTT):
                for cg in range(4):
                    s.wait_ge(p6.trs, ntr + 1)
                    s.copy(x_f[:, :].rearrange("p (c t) -> p c t", c=NCH)
                           [:, cg * 4:(cg + 1) * 4, tt * P:(tt + 1) * P],
                           p6.ps[ntr % 2][:, :].rearrange("p (c4 t) -> p c4 t", c4=4)
                           ).then_inc(p6.xfs, 1)
                    ntr += 1
            for th in range(2):
                tsl = slice(th * 512, (th + 1) * 512)
                s.wait_ge(p6.rsm, 2 * th + 1)
                s.copy(p6.rsrow[0:1, tsl], p6.prow[0:1, :]).then_inc(p6.rse, 1)
                for hc in range(NHC):
                    qi = th * NHC + hc
                    s.wait_ge(p6.mm, NGP6 + NWO6 + qi + 1)
                    if qi >= 2:
                        s.wait_ge(p6.a2s, qi - 1)
                    s.activation(p6.sgh[qi % 2][:, :], p6.ps[qi % 2][:, :], AF.Silu,
                                 scale=p6.sgc[:, hc:hc + 1]).then_inc(p6.gup, 1)
                    s.activation(p6.a2sb[:, hc * 512:(hc + 1) * 512], p6.pu[qi % 2][:, :],
                                 AF.Copy, scale=p6.suc[:, hc:hc + 1]).then_inc(p6.gup, 1)
                s.wait_ge(p6.rsm, 2 * th + 2)
                s.copy(p6.csrow[0:1, tsl], p6.prow[0:1, :]).then_inc(p6.rse, 1)
                for dc in range(NCH):
                    di = th * NCH + dc
                    s.wait_ge(p6.dmm, di + 1)
                    if di >= 2:
                        s.wait_ge(p6.dtr, di - 1)
                    s.activation(p6.yfb[di % 2][:, :], p6.pu[di % 2][:, :], AF.Copy,
                                 scale=p6.sdc[:, dc:dc + 1]).then_inc(p6.dse, 1)
            s.wait_ge(p6.dev, 32)
            for tt in range(NTT):
                ln_scalar(p6, tt, 56, 32)

    ctx.close()
    return nc


from concourse.bass_utils import run_bass_kernel_spmd
import ml_dtypes

_BF = ml_dtypes.bfloat16
_cc = np.ascontiguousarray


def _planar_perm():
    perm = np.empty(2048, np.int64)
    for h in range(32):
        for i in range(64):
            perm[h * 64 + i] = h * 64 + (2 * i if i < 32 else 2 * (i - 32) + 1)
    return perm


def kernel(L0_attn_x, L0_attn_w_attn, L0_attn_b_attn, L0_attn_w_proj, L0_attn_b_proj,
           L0_ffn_gate_w, L0_ffn_w1, L0_ffn_w2, L0_ffn_w3,
           L1_attn_Wq, L1_attn_Wk, L1_attn_Wv, L1_attn_Wg, L1_attn_Wgk, L1_attn_Wo,
           L1_ffn_wq_gate, L1_ffn_s_gate, L1_ffn_wq_up, L1_ffn_s_up,
           L1_ffn_wq_down, L1_ffn_s_down):
    x = np.asarray(L0_attn_x, np.float32)
    perm = _planar_perm()
    weights = {
        "w_attn": np.asarray(L0_attn_w_attn, np.float32),
        "b_attn": np.asarray(L0_attn_b_attn, np.float32),
        "w_proj": np.asarray(L0_attn_w_proj, np.float32),
        "b_proj": np.asarray(L0_attn_b_proj, np.float32),
        "gate_w": np.asarray(L0_ffn_gate_w, np.float32),
        "w1": np.asarray(L0_ffn_w1, np.float32),
        "w2": np.asarray(L0_ffn_w2, np.float32),
        "w3": np.asarray(L0_ffn_w3, np.float32),
        "WqT": _cc(np.asarray(L1_attn_Wq, np.float32).T[:, perm]),
        "WkT": _cc(np.asarray(L1_attn_Wk, np.float32).T[:, perm]),
        "WvT": _cc(np.asarray(L1_attn_Wv, np.float32).T),
        "WgT": _cc(np.asarray(L1_attn_Wg, np.float32).T),
        "WgkT": _cc(np.asarray(L1_attn_Wgk, np.float32).T),
        "WoT": _cc(np.asarray(L1_attn_Wo, np.float32).T),
        "wqgT": _cc(np.asarray(L1_ffn_wq_gate, np.int32).T),
        "wquT": _cc(np.asarray(L1_ffn_wq_up, np.int32).T),
        "wqdT": _cc(np.asarray(L1_ffn_wq_down, np.int32).T),
        "sg2d": _cc(np.asarray(L1_ffn_s_gate, np.float32).reshape(44, 128).T),
        "su2d": _cc(np.asarray(L1_ffn_s_up, np.float32).reshape(44, 128).T),
        "sd2d": _cc(np.asarray(L1_ffn_s_down, np.float32).reshape(16, 128).T),
    }
    id32 = np.eye(128, dtype=np.float32)
    consts = {
        "id32": id32,
        "idbf": id32.astype(_BF),
        "maskneg": np.triu(np.full((128, 128), -1e30, np.float32), 1),
        "maskbf": np.tril(np.ones((128, 128), np.float32)).astype(_BF),
    }
    inv = 1.0 / 10000.0 ** (np.arange(0, 64, 2) / 64.0)
    in_maps = []
    for c in range(8):
        b, half = c // 2, c % 2
        sl = slice(half * 1024, half * 1024 + 1024)
        pos = np.arange(half * 1024, half * 1024 + 1024, dtype=np.float64)
        fr = pos[:, None] * inv[None, :]
        cosm = np.concatenate([np.cos(fr), np.cos(fr)], 1).astype(np.float32)
        sinm = np.concatenate([np.sin(fr), np.sin(fr)], 1).astype(np.float32)
        m = {"x_f": _cc(x[b, sl].T), "x_t": _cc(x[b, sl]),
             "coreinfo": np.array([[1024 * half, 1024 * half, 1024 * half,
                                    16 * half]], np.int32),
             "rope_cos": cosm, "rope_sin": sinm}
        m.update(weights)
        m.update(consts)
        in_maps.append(m)
    nc = build_fused()
    res = run_bass_kernel_spmd(nc, in_maps, core_ids=list(range(8)))
    out = np.empty((4, 2048, 2048), np.float32)
    for c in range(8):
        b, half = c // 2, c % 2
        out[b, half * 1024:half * 1024 + 1024] = res.results[c]["y"]
    return out


# version 09 (final)
# speedup vs baseline: 1.0148x; 1.0148x over previous
import contextlib
from types import SimpleNamespace as NS
import numpy as np
import concourse.bass as bass
import concourse.mybir as mybir

F32 = mybir.dt.float32
BF16 = mybir.dt.bfloat16
I32 = mybir.dt.int32
AF = mybir.ActivationFunctionType
ALU = mybir.AluOpType
AX = mybir.AxisListType

P = 128
D = 2048
NCH = 16
TOK = 1024
NTT = 8
T = 2048
EPS_LN = 1e-5
EPS_RMS = 1e-6
PIECES = [(0, 512), (512, 256)]


def chunks_of(qt):
    klen = (qt + 1) * P
    out = []
    kc = 0
    while kc * 512 < klen:
        out.append((kc, min(512, klen - kc * 512)))
        kc += 1
    return out


def attn_sched(npairs):
    sched = []
    c_smm = c_mask = c_exp = c_zsc = c_kt = c_oev = 0
    for p in range(npairs):
        for g in range(4):
            rec = {"smm0": c_smm, "mask0": c_mask, "exp0": c_exp,
                   "zsc0": c_zsc, "kt0": c_kt, "oev0": c_oev}
            for i in range(4):
                qt = 4 * g + i
                n = len(chunks_of(qt))
                c_smm += n
                c_exp += n
                c_mask += 1
                c_zsc += 1
            c_kt += 4 * g + 4
            c_oev += 1
            rec.update({"smm1": c_smm, "mask1": c_mask, "exp1": c_exp,
                        "zsc1": c_zsc, "kt1": c_kt, "oev1": c_oev})
            sched.append(rec)
    return sched


def build_fused():
    nc = bass.Bass("TRN2", target_bir_lowering=False)

    x_f_in = nc.dram_tensor("x_f", [D, TOK], F32, kind="ExternalInput")
    x_t_in = nc.dram_tensor("x_t", [TOK, D], F32, kind="ExternalInput")
    coreinfo = nc.dram_tensor("coreinfo", [1, 4], I32, kind="ExternalInput")
    w_attn = nc.dram_tensor("w_attn", [D, 3 * D], F32, kind="ExternalInput")
    b_attn = nc.dram_tensor("b_attn", [3 * D], F32, kind="ExternalInput")
    w_proj = nc.dram_tensor("w_proj", [D, D], F32, kind="ExternalInput")
    b_proj = nc.dram_tensor("b_proj", [D], F32, kind="ExternalInput")
    gate_w = nc.dram_tensor("gate_w", [D, 8], F32, kind="ExternalInput")
    w1 = nc.dram_tensor("w1", [8, D, 768], F32, kind="ExternalInput")
    w2 = nc.dram_tensor("w2", [8, 768, D], F32, kind="ExternalInput")
    w3 = nc.dram_tensor("w3", [8, D, 768], F32, kind="ExternalInput")
    WqT = nc.dram_tensor("WqT", [D, D], F32, kind="ExternalInput")
    WkT = nc.dram_tensor("WkT", [D, D], F32, kind="ExternalInput")
    WvT = nc.dram_tensor("WvT", [D, D], F32, kind="ExternalInput")
    WgkT = nc.dram_tensor("WgkT", [D, 32], F32, kind="ExternalInput")
    WgT = nc.dram_tensor("WgT", [D, D], F32, kind="ExternalInput")
    WoT = nc.dram_tensor("WoT", [D, D], F32, kind="ExternalInput")
    wqgT = nc.dram_tensor("wqgT", [D, 5632], I32, kind="ExternalInput")
    wquT = nc.dram_tensor("wquT", [D, 5632], I32, kind="ExternalInput")
    wqdT = nc.dram_tensor("wqdT", [5632, D], I32, kind="ExternalInput")
    sg2d = nc.dram_tensor("sg2d", [P, 44], F32, kind="ExternalInput")
    su2d = nc.dram_tensor("su2d", [P, 44], F32, kind="ExternalInput")
    sd2d = nc.dram_tensor("sd2d", [P, NCH], F32, kind="ExternalInput")
    rope_cos = nc.dram_tensor("rope_cos", [TOK, 64], F32, kind="ExternalInput")
    rope_sin = nc.dram_tensor("rope_sin", [TOK, 64], F32, kind="ExternalInput")
    id32_in = nc.dram_tensor("id32", [P, P], F32, kind="ExternalInput")
    idbf_in = nc.dram_tensor("idbf", [P, P], BF16, kind="ExternalInput")
    maskneg_in = nc.dram_tensor("maskneg", [P, P], F32, kind="ExternalInput")
    maskbf_in = nc.dram_tensor("maskbf", [P, P], BF16, kind="ExternalInput")
    y_out = nc.dram_tensor("y", [TOK, D], F32, kind="ExternalOutput")

    A0p_own = [nc.dram_tensor(f"A0own{i}", [TOK, D], BF16) for i in range(3)]
    A0p_o = [nc.dram_tensor(f"A0o{i}", [T, D], BF16) for i in range(3)]
    B0own = nc.dram_tensor("B0own", [TOK, D], BF16)
    B0o = nc.dram_tensor("B0o", [T, D], BF16)
    A1p_own = [nc.dram_tensor(f"A1own{i}", [TOK, D], BF16) for i in range(3)]
    A1p_o = [nc.dram_tensor(f"A1o{i}", [T, D], BF16) for i in range(3)]
    Agown = nc.dram_tensor("Agown", [TOK, 32], F32)
    Ago = nc.dram_tensor("Ago", [T, 32], F32)
    B1h_own = [nc.dram_tensor(f"B1own{i}", [TOK // 2, D], BF16) for i in range(2)]
    B1h_o = [nc.dram_tensor(f"B1o{i}", [TOK, D], BF16) for i in range(2)]
    gdram = nc.dram_tensor("gdram", [32, T], F32)
    A0loc = nc.dram_tensor("A0loc", [T, 3 * TOK], BF16)
    A1loc = nc.dram_tensor("A1loc", [T, 3 * TOK], BF16)
    B1loc = nc.dram_tensor("B1loc", [T, TOK], BF16)
    gloc = nc.dram_tensor("gloc", [16, T], F32)
    RG = [[0, 1], [2, 3], [4, 5], [6, 7]]

    ctx = contextlib.ExitStack()
    E = ctx.enter_context

    stream = E(nc.sbuf_tensor("stream", [P, NTT * D], F32))
    x_f = E(nc.sbuf_tensor("x_fb", [P, NCH * TOK], BF16))
    isb = E(nc.sbuf_tensor("isb", [1, 4], I32))
    i32sb = E(nc.sbuf_tensor("i32sb", [P, P], F32))
    ibfsb = E(nc.sbuf_tensor("ibfsb", [P, P], BF16))
    mnegsb = E(nc.sbuf_tensor("mnegsb", [P, P], F32))
    mbfsb = E(nc.sbuf_tensor("mbfsb", [P, P], BF16))
    ones1 = E(nc.sbuf_tensor("ones1", [1, P], BF16))
    onec32 = E(nc.sbuf_tensor("onec32", [1, P], F32))
    nones = E(nc.sbuf_tensor("nones", [P, 1], BF16))
    epsln = E(nc.sbuf_tensor("epsln", [P, 1], F32))
    epsrm = E(nc.sbuf_tensor("epsrm", [P, 1], F32))
    shcol = E(nc.sbuf_tensor("shcol", [P, 1], F32))
    sq = E(nc.sbuf_tensor("sq", [P, NTT], F32))
    sums = E(nc.sbuf_tensor("sums", [P, NTT], F32))
    mu = E(nc.sbuf_tensor("mu", [P, NTT], F32))
    varv = E(nc.sbuf_tensor("varv", [P, NTT], F32))
    rsd = E(nc.sbuf_tensor("rsd", [P, NTT], F32))
    scr = E(nc.sbuf_tensor("scr", [P, D], F32))
    sgc_p = E(nc.sbuf_tensor("sgc_p", [P, 44], F32))
    suc_p = E(nc.sbuf_tensor("suc_p", [P, 44], F32))
    sdc_p = E(nc.sbuf_tensor("sdc_p", [P, NCH], F32))

    ccp = [E(nc.semaphore(f"ccp{i}")) for i in range(3)]
    cpp = [E(nc.semaphore(f"cpp{i}")) for i in range(3)]
    cc1 = E(nc.semaphore("cc1"))
    cc3 = E(nc.semaphore("cc3"))
    cst = E(nc.semaphore("cst"))

    sc = contextlib.ExitStack()
    p1 = NS()
    p1.wblk = [sc.enter_context(nc.sbuf_tensor(f"p1_wblk{i}", [P, NCH * 512], BF16)) for i in range(2)]
    p1.ev = [sc.enter_context(nc.sbuf_tensor(f"p1_ev{i}", [P, NTT * 512], BF16)) for i in range(2)]
    p1.brow = sc.enter_context(nc.sbuf_tensor("p1_brow", [1, 3 * D], BF16))
    p1.ps = [sc.enter_context(nc.psum_tensor(f"p1_ps{i}", [P, 512], F32)) for i in range(2)]
    _s = E(nc.semaphore("p1_ldw"))
    p1.ldw = [_s, _s]
    p1.ldx = E(nc.semaphore("p1_ldx"))
    p1.mm = E(nc.semaphore("p1_mm"))
    p1.evs = E(nc.semaphore("p1_evs"))
    _s = E(nc.semaphore("p1_so"))
    p1.so = [_s, _s]
    sc.close()
    sc = contextlib.ExitStack()
    p2 = NS()
    p2.qsb = [sc.enter_context(nc.sbuf_tensor(f"p2_qsb{i}", [P, T], BF16)) for i in range(2)]
    p2.ksb = [sc.enter_context(nc.sbuf_tensor(f"p2_ksb{i}", [P, T], BF16)) for i in range(2)]
    p2.vsb = [sc.enter_context(nc.sbuf_tensor(f"p2_vsb{i}", [P, 16 * P], BF16)) for i in range(2)]
    p2.psb = [sc.enter_context(nc.sbuf_tensor(f"p2_psb{i}", [P, 4 * T], BF16)) for i in range(2)]
    p2.ptsb = [sc.enter_context(nc.sbuf_tensor(f"p2_ptsb{i}", [P, 512], BF16)) for i in range(2)]
    p2.zp = sc.enter_context(nc.sbuf_tensor("p2_zp", [P, 16], F32))
    p2.zinv = sc.enter_context(nc.sbuf_tensor("p2_zinv", [P, 4], F32))
    p2.osb = [sc.enter_context(nc.sbuf_tensor(f"p2_osb{i}", [P, T], BF16)) for i in range(2)]
    p2.ps_s = [sc.enter_context(nc.psum_tensor(f"p2_ps_s{i}", [P, 512], F32)) for i in range(4)]
    p2.pt_ps = [sc.enter_context(nc.psum_tensor(f"p2_pt_ps{i}", [P, 1024], BF16)) for i in range(2)]
    p2.o_ps = sc.enter_context(nc.psum_tensor("p2_o_ps", [P, 512], F32))
    for nm in ["qld", "ssem", "msem", "esem", "zsem", "tsem", "ptsem", "avsem",
               "osem", "vself"]:
        setattr(p2, nm, E(nc.semaphore(f"p2_{nm}")))
    _s = E(nc.semaphore("p2_sto"))
    p2.sto = [_s, _s]
    p2.sched = attn_sched(8)
    sc.close()
    sc = contextlib.ExitStack()
    p3 = NS()
    p3.gwsb = sc.enter_context(nc.sbuf_tensor("p3_gwsb", [P, NCH * 8], BF16))
    p3.bprow = sc.enter_context(nc.sbuf_tensor("p3_bprow", [1, D], BF16))
    p3.gexp = sc.enter_context(nc.sbuf_tensor("p3_gexp", [P, NTT * 8], F32))
    p3.comb = sc.enter_context(nc.sbuf_tensor("p3_comb", [P, NTT * 8], F32))
    p3.e8a = sc.enter_context(nc.sbuf_tensor("p3_e8a", [P, 8], F32))
    p3.t2a = sc.enter_context(nc.sbuf_tensor("p3_t2a", [P, 1], F32))
    p3.t2b = sc.enter_context(nc.sbuf_tensor("p3_t2b", [P, 1], F32))
    scb = contextlib.ExitStack()
    p3.ofb = scb.enter_context(nc.sbuf_tensor("p3_ofb", [P, NCH * TOK], BF16))
    p3.wpb = [scb.enter_context(nc.sbuf_tensor(f"p3_wpb{i}", [P, NCH * 512], BF16)) for i in range(2)]
    scb.close()
    p3.w13b = [sc.enter_context(nc.sbuf_tensor(f"p3_w13b{i}", [P, NCH * 512], BF16)) for i in range(4)]
    p3.w2b = [sc.enter_context(nc.sbuf_tensor(f"p3_w2b{i}", [P, 6 * 512], BF16)) for i in range(2)]
    p3.gs = [sc.enter_context(nc.sbuf_tensor(f"p3_gs{i}", [P, 512], F32)) for i in range(2)]
    p3.a2t = [sc.enter_context(nc.sbuf_tensor(f"p3_a2t{i}", [P, 512], BF16)) for i in range(2)]
    p3.a2f = sc.enter_context(nc.sbuf_tensor("p3_a2f", [P, NTT * 6 * P], BF16))
    p3.ps = [sc.enter_context(nc.psum_tensor(f"p3_ps{i}", [P, 512], F32)) for i in range(2)]
    p3.pg = [sc.enter_context(nc.psum_tensor(f"p3_pg{i}", [P, 512], F32)) for i in range(2)]
    p3.pu = [sc.enter_context(nc.psum_tensor(f"p3_pu{i}", [P, 512], F32)) for i in range(2)]
    p3.ptr = [sc.enter_context(nc.psum_tensor(f"p3_ptr{i}", [P, 1024], BF16)) for i in range(2)]
    p3.pxf = p3.pu
    for nm in ["ldx", "ldc", "mm", "evs", "lnv", "lns", "trs", "xfs", "gsem",
               "vch", "sil", "a2s", "a2e", "dmm", "dev"]:
        setattr(p3, nm, E(nc.semaphore(f"p3_{nm}")))
    _s = E(nc.semaphore("p3_ldw")); p3.ldw = [_s, _s]
    _s = E(nc.semaphore("p3_lde")); p3.lde = [_s, _s]
    _s = E(nc.semaphore("p3_ld2")); p3.ld2 = [_s, _s]
    sc.close()
    sc = contextlib.ExitStack()
    p4 = NS()
    p4.wblk = [sc.enter_context(nc.sbuf_tensor(f"p4_wblk{i}", [P, NCH * 512], BF16)) for i in range(2)]
    p4.wgk = sc.enter_context(nc.sbuf_tensor("p4_wgk", [P, NCH * 32], BF16))
    p4.stage = sc.enter_context(nc.sbuf_tensor("p4_stage", [P, NTT * D], BF16))
    p4.cosb = sc.enter_context(nc.sbuf_tensor("p4_cosb", [P, NTT * 64], BF16))
    p4.sinb = sc.enter_context(nc.sbuf_tensor("p4_sinb", [P, NTT * 64], BF16))
    p4.tmp = [sc.enter_context(nc.sbuf_tensor(f"p4_tmp{i}", [P, 1024], BF16)) for i in range(4)]
    p4.gks = sc.enter_context(nc.sbuf_tensor("p4_gks", [P, NTT * 32], F32))
    p4.ps = [sc.enter_context(nc.psum_tensor(f"p4_ps{i}", [P, 512], F32)) for i in range(2)]
    p4.pgk = sc.enter_context(nc.psum_tensor("p4_pgk", [P, 32], F32))
    for nm in ["ldc", "mm", "mgk", "esem", "gsig", "vch", "sto", "trs", "xfs"]:
        setattr(p4, nm, E(nc.semaphore(f"p4_{nm}")))
    _s = E(nc.semaphore("p4_ldw")); p4.ldw = [_s, _s]
    sc.close()
    sc = contextlib.ExitStack()
    p5 = NS()
    HD5 = 64
    p5.qsb = [sc.enter_context(nc.sbuf_tensor(f"p5_qsb{i}", [P, T], BF16)) for i in range(2)]
    p5.ksb = [sc.enter_context(nc.sbuf_tensor(f"p5_ksb{i}", [P, T], BF16)) for i in range(2)]
    p5.vsb = [sc.enter_context(nc.sbuf_tensor(f"p5_vsb{i}", [P, 16 * P], BF16)) for i in range(2)]
    p5.gsb = sc.enter_context(nc.sbuf_tensor("p5_gsb", [P, 16 * 32], F32))
    p5.gz = sc.enter_context(nc.sbuf_tensor("p5_gz", [32, 1], F32))
    p5.grow = [sc.enter_context(nc.sbuf_tensor(f"p5_grow{i}", [1, T], F32)) for i in range(2)]
    p5.gq16 = sc.enter_context(nc.sbuf_tensor("p5_gq16", [P, 16], F32))
    p5.gkbc = sc.enter_context(nc.sbuf_tensor("p5_gkbc", [P, T], F32))
    p5.esb = [sc.enter_context(nc.sbuf_tensor(f"p5_esb{i}", [P, 512], BF16)) for i in range(2)]
    p5.ptsb = [sc.enter_context(nc.sbuf_tensor(f"p5_ptsb{i}", [P, 512], BF16)) for i in range(2)]
    p5.osb = [sc.enter_context(nc.sbuf_tensor(f"p5_osb{i}", [HD5, T], BF16)) for i in range(2)]
    _scg = contextlib.ExitStack()
    p5.gf = _scg.enter_context(nc.sbuf_tensor("p5_gf", [32, T], F32))
    p5.gcum = _scg.enter_context(nc.sbuf_tensor("p5_gcum", [32, T], F32))
    _scg.close()
    p5.asb = [sc.enter_context(nc.sbuf_tensor(f"p5_asb{i}", [P, 4 * T], BF16)) for i in range(2)]
    p5.ps_s = [sc.enter_context(nc.psum_tensor(f"p5_ps_s{i}", [P, 512], F32)) for i in range(4)]
    p5.pt_ps = [sc.enter_context(nc.psum_tensor(f"p5_pt_ps{i}", [P, 1024], BF16)) for i in range(2)]
    p5.o_ps = sc.enter_context(nc.psum_tensor("p5_o_ps", [P, 512], F32))
    p5.pbc = sc.enter_context(nc.psum_tensor("p5_pbc", [P, 512], F32))
    for nm in ["qld", "gsc", "gtr", "gqt", "gqe", "bcs", "bce", "ssem", "esem",
               "asem", "tsem", "ptsem", "ptsem_s", "avsem", "osem", "gst", "vsf", "gqs"]:
        setattr(p5, nm, E(nc.semaphore(f"p5_{nm}")))
    _s = E(nc.semaphore("p5_gld")); p5.gld = [_s, _s]
    _s = E(nc.semaphore("p5_sto")); p5.sto = [_s, _s]
    p5.sched = attn_sched(16)
    sc.close()
    sc = contextlib.ExitStack()
    p6 = NS()
    NHC = 44
    p6.sgc = sgc_p
    p6.suc = suc_p
    p6.sdc = sdc_p
    p6.hvar = sc.enter_context(nc.sbuf_tensor("p6_hvar", [P, 32], F32))
    p6.rsrow = sc.enter_context(nc.sbuf_tensor("p6_rsrow", [1, TOK], BF16))
    p6.csrow = sc.enter_context(nc.sbuf_tensor("p6_csrow", [1, TOK], BF16))
    scb = contextlib.ExitStack()
    p6.ogf = scb.enter_context(nc.sbuf_tensor("p6_ogf", [P, NCH * TOK], BF16))
    p6.sg_sb = scb.enter_context(nc.sbuf_tensor("p6_sg_sb", [P, NTT * D], BF16))
    p6.otb = [scb.enter_context(nc.sbuf_tensor(f"p6_otb{i}", [P, D], BF16)) for i in range(2)]
    p6.wb = [scb.enter_context(nc.sbuf_tensor(f"p6_wb{i}", [P, NCH * 256], BF16)) for i in range(2)]
    scb.close()
    p6.a2sb = sc.enter_context(nc.sbuf_tensor("p6_a2sb", [P, NHC * 512], BF16))
    p6.wqg_b = [sc.enter_context(nc.sbuf_tensor(f"p6_wqg_b{i}", [P, NCH * P], BF16)) for i in range(2)]
    p6.wqu_b = [sc.enter_context(nc.sbuf_tensor(f"p6_wqu_b{i}", [P, NCH * P], BF16)) for i in range(2)]
    p6.wqd_b = [sc.enter_context(nc.sbuf_tensor(f"p6_wqd_b{i}", [P, NHC * P], BF16)) for i in range(2)]
    p6.sgh = [sc.enter_context(nc.sbuf_tensor(f"p6_sgh{i}", [P, 512], BF16)) for i in range(2)]
    p6.yfb = [sc.enter_context(nc.sbuf_tensor(f"p6_yfb{i}", [P, 512], BF16)) for i in range(2)]
    p6.ps = [sc.enter_context(nc.psum_tensor(f"p6_ps{i}", [P, 512], F32)) for i in range(2)]
    p6.pu = [sc.enter_context(nc.psum_tensor(f"p6_pu{i}", [P, 512], F32)) for i in range(2)]
    p6.ptrb = [sc.enter_context(nc.psum_tensor(f"p6_ptr{i}", [P, 1024], BF16)) for i in range(2)]
    p6.prow = sc.enter_context(nc.psum_tensor("p6_prow", [P, 512], F32))
    for nm in ["ldm", "mm", "evs", "lnv", "lns", "onr", "sil", "trs", "xfs",
               "rsm", "rse", "gup", "a2s", "dmm", "dse", "dtr", "dev", "sto"]:
        setattr(p6, nm, E(nc.semaphore(f"p6_{nm}")))
    _s = E(nc.semaphore("p6_ldo")); p6.ldo = [_s, _s]
    _s = E(nc.semaphore("p6_ldw")); p6.ldw = [_s, _s]
    _s = E(nc.semaphore("p6_ldq")); p6.ldq = [_s, _s]
    _s = E(nc.semaphore("p6_ldd")); p6.ldd = [_s, _s]
    sc.close()

    NGP6 = 64
    NWO6 = 64

    def mm_gu3(gidx):
        return 40 + gidx + 1

    NHC = 44
    with nc.Block() as block:

        @block.sync
        def _(sy):
            sy.dma_start(out=isb[:, :], in_=coreinfo[:, :]).then_inc(cst, 16)
            sy.dma_start(out=i32sb[:, :], in_=id32_in[:, :]).then_inc(cst, 16)
            sy.dma_start(out=ibfsb[:, :], in_=idbf_in[:, :]).then_inc(cst, 16)
            sy.dma_start(out=mnegsb[:, :], in_=maskneg_in[:, :]).then_inc(cst, 16)
            sy.dma_start(out=mbfsb[:, :], in_=maskbf_in[:, :]).then_inc(cst, 16)
            sy.dma_start(out=stream[:, :].rearrange("p (tt d) -> p tt d", tt=NTT),
                         in_=x_t_in.ap().rearrange("(tt p) d -> p tt d", p=P)
                         ).then_inc(p3.ldx, 16)
            sy.dma_start(out=p6.sgc[:, :], in_=sg2d[:, :]).then_inc(p6.ldm, 16)
            sy.dma_start(out=p6.suc[:, :], in_=su2d[:, :]).then_inc(p6.ldm, 16)
            sy.dma_start(out=p6.sdc[:, :], in_=sd2d[:, :]).then_inc(p6.ldm, 16)
            sy.wait_ge(cst, 81)
            rhb0 = sy.alloc_register("rhb0")
            rhb1 = sy.alloc_register("rhb1")
            rtok = sy.alloc_register("rtok")
            sy.load(rhb0, isb[0:1, 0:1])
            sy.load(rhb1, isb[0:1, 1:2])
            sy.load(rtok, isb[0:1, 2:3])
            vhb0 = sy.snap(rhb0, min_val=0, max_val=1024)
            vhb1 = sy.snap(rhb1, min_val=0, max_val=1024)
            vtok = sy.snap(rtok, min_val=0, max_val=1024)

            A0rs = [t_.ap().rearrange("(tt p) o -> p tt o", p=P) for t_ in A0p_own]
            for ob in range(12):
                sy.wait_ge(p1.evs, 8 * (ob + 1))
                if ob >= 1:
                    sy.wait_ge(p1.so[0], 16 * ob)
                sy.dma_start(out=A0rs[ob // 4][:, :, (ob % 4) * 512:(ob % 4 + 1) * 512],
                             in_=p1.ev[ob % 2][:, :].rearrange("p (tt o) -> p tt o", tt=NTT)
                             ).then_inc(p1.so[0], 16)

            for i in range(3):
                sy.wait_ge(ccp[i], 1)
                sy.dma_start(out=A0loc[:, i * TOK:(i + 1) * TOK],
                             in_=A0p_o[i][:, bass.ds(vhb0, TOK)]).then_inc(cpp[i], 16)
            for h in range(8):
                if h >= 1:
                    sy.wait_ge(p2.qld, 48 * h)
                if h >= 2:
                    sy.wait_ge(p2.avsem, p2.sched[(h - 2) * 4 + 3]["kt1"])
                else:
                    sy.wait_ge(cpp[0], 16)
                sy.dma_start(out=p2.qsb[h % 2][:, :],
                             in_=A0loc[:, h * P:(h + 1) * P],
                             transpose=True).then_inc(p2.qld, 16)
                if h < 2:
                    sy.wait_ge(cpp[1], 16)
                sy.dma_start(out=p2.ksb[h % 2][:, :],
                             in_=A0loc[:, TOK + h * P:TOK + (h + 1) * P],
                             transpose=True).then_inc(p2.qld, 16)
                if h < 2:
                    sy.wait_ge(cpp[2], 16)
                sy.dma_start(out=p2.vsb[h % 2][:, :].rearrange("p (kt d) -> p kt d", kt=16),
                             in_=A0loc.ap().rearrange("(kt p) o -> p kt o", p=P)
                             [:, :, 2 * TOK + h * P:2 * TOK + (h + 1) * P]
                             ).then_inc(p2.qld, 16)

            sy.wait_ge(cc1, 1)
            sy.dma_start(out=p3.ofb[:, :].rearrange("p (c t) -> p c t", c=NCH),
                         in_=B0o.ap().rearrange("(c p) t -> p c t", p=P)
                         [:, :, bass.ds(vtok, TOK)]).then_inc(p3.ldx, 16)

            sy.wait_ge(p4.gsig, 2 * NTT)
            sy.dma_start(out=Agown.ap().rearrange("(tt p) e -> p tt e", p=P),
                         in_=p4.gks[:, :].rearrange("p (tt e) -> p tt e", tt=NTT)
                         ).then_inc(p4.sto, 16)
            A1rs = [t_.ap().rearrange("(tt p) o -> p tt o", p=P) for t_ in A1p_own]
            for mi in range(3):
                if mi < 2:
                    sy.wait_ge(p4.vch, 48 * (mi + 1))
                else:
                    sy.wait_ge(p4.esem, 96)
                sy.wait_ge(p4.sto, 16 * (mi + 1))
                sy.dma_start(out=A1rs[mi][:, :, :],
                             in_=p4.stage[:, :].rearrange("p (tt o) -> p tt o", tt=NTT)
                             ).then_inc(p4.sto, 16)

            for i in range(3):
                sy.wait_ge(ccp[i], 2)
                sy.dma_start(out=A1loc[:, i * TOK:(i + 1) * TOK],
                             in_=A1p_o[i][:, bass.ds(vhb1, TOK)]).then_inc(cpp[i], 16)
            for prr in range(8):
                if prr >= 1:
                    sy.wait_ge(p5.qld, 48 * prr)
                if prr >= 2:
                    sy.wait_ge(p5.avsem, p5.sched[(2 * (prr - 2) + 1) * 4 + 3]["kt1"])
                else:
                    sy.wait_ge(cpp[0], 32)
                sy.dma_start(out=p5.qsb[prr % 2][:, :],
                             in_=A1loc[:, prr * P:(prr + 1) * P],
                             transpose=True).then_inc(p5.qld, 16)
                if prr < 2:
                    sy.wait_ge(cpp[1], 32)
                sy.dma_start(out=p5.ksb[prr % 2][:, :],
                             in_=A1loc[:, TOK + prr * P:TOK + (prr + 1) * P],
                             transpose=True).then_inc(p5.qld, 16)
                if prr < 2:
                    sy.wait_ge(cpp[2], 32)
                sy.dma_start(out=p5.vsb[prr % 2][:, :].rearrange("p (kt d) -> p kt d", kt=16),
                             in_=A1loc.ap().rearrange("(kt p) o -> p kt o", p=P)
                             [:, :, 2 * TOK + prr * P:2 * TOK + (prr + 1) * P]
                             ).then_inc(p5.qld, 16)

            sy.wait_ge(cc3, 1)
            sy.dma_start(out=B1loc[0:512, :],
                         in_=B1h_o[0][0:512, bass.ds(vtok, TOK)]).then_inc(cc3, 16)
            sy.dma_start(out=B1loc[1024:1536, :],
                         in_=B1h_o[0][512:1024, bass.ds(vtok, TOK)]).then_inc(cc3, 16)
            sy.wait_ge(cc3, 34)
            sy.dma_start(out=B1loc[512:1024, :],
                         in_=B1h_o[1][0:512, bass.ds(vtok, TOK)]).then_inc(cc3, 16)
            sy.dma_start(out=B1loc[1536:2048, :],
                         in_=B1h_o[1][512:1024, bass.ds(vtok, TOK)]).then_inc(cc3, 16)
            for tt in range(NTT):
                if tt >= 1:
                    sy.wait_ge(p6.ldo[0], 16 * tt)
                if tt >= 2:
                    sy.wait_ge(p6.onr, 5 * (tt - 1))
                else:
                    sy.wait_ge(cc3, 66)
                sy.dma_start(out=p6.otb[tt % 2][:, :],
                             in_=B1loc[:, tt * P:(tt + 1) * P],
                             transpose=True).then_inc(p6.ldo[0], 16)
            for tt in range(NTT):
                sy.wait_ge(p6.lnv, 56 + 7 * (tt + 1))
                sy.dma_start(out=y_out.ap().rearrange("(tt p) d -> p tt d", p=P)[:, tt, :],
                             in_=stream[:, tt * D:(tt + 1) * D]).then_inc(p6.sto, 16)
            sy.wait_ge(p6.sto, 16 * NTT)

        @block.gpsimd
        def _(g):
            g.memset(ones1[:, :], 1.0)
            g.memset(onec32[:, :], 1.0)
            g.memset(nones[:, :], -128.0)
            g.memset(epsln[:, :], EPS_LN)
            g.memset(epsrm[:, :], EPS_RMS)
            g.memset(p5.gz[:, :], 0.0)
            g.memset(shcol[:, :], -16.0).then_inc(cst, 1)
            g.dma_start(out=x_f[:, :].rearrange("p (c t) -> p c t", c=NCH),
                        in_=x_f_in.ap().rearrange("(c p) t -> p c t", p=P)
                        ).then_inc(p1.ldx, 16)
            g.dma_start(out=p1.brow[:, :], in_=b_attn.ap().unsqueeze(0)).then_inc(p1.ldx, 16)
            for ob in range(12):
                if ob >= 1:
                    g.wait_ge(p1.ldw[0], 16 * ob)
                if ob >= 2:
                    g.wait_ge(p1.mm, 8 * (ob - 1))
                g.dma_start(out=p1.wblk[ob % 2][:, :].rearrange("p (c o) -> p c o", c=NCH),
                            in_=w_attn.ap().rearrange("(c p) o -> p c o", p=P)
                            [:, :, ob * 512:(ob + 1) * 512]).then_inc(p1.ldw[0], 16)
                if ob == 5 or ob == 9:
                    i = (ob - 5) // 4
                    g.wait_ge(p1.so[0], 16 * 4 * (i + 1))
                    g.collective_compute("AllGather", ALU.bypass, replica_groups=RG,
                                         ins=[A0p_own[i].ap().opt()],
                                         outs=[A0p_o[i].ap().opt()]
                                         ).then_inc(ccp[i], 1)
            g.wait_ge(p1.so[0], 16 * 12)
            g.collective_compute("AllGather", ALU.bypass, replica_groups=RG,
                                 ins=[A0p_own[2].ap().opt()], outs=[A0p_o[2].ap().opt()]
                                 ).then_inc(ccp[2], 1)
            for h in range(8):
                g.wait_ge(p2.osem, 4 * (h + 1))
                if h >= 1:
                    g.wait_ge(p2.sto[0], 16 * h)
                g.dma_start(out=B0own[h * P:(h + 1) * P, :],
                            in_=p2.osb[h % 2][:, :]).then_inc(p2.sto[0], 16)
            g.wait_ge(p2.sto[0], 16 * 8)
            g.collective_compute("AllGather", ALU.bypass, replica_groups=RG,
                                 ins=[B0own.ap().opt()], outs=[B0o.ap().opt()]
                                 ).then_inc(cc1, 1)
            g.wait_ge(cc1, 1)
            g.dma_start(out=p3.bprow[:, :], in_=b_proj.ap().unsqueeze(0)).then_inc(p3.ldc, 16)
            g.dma_start(out=p3.gwsb[:, :].rearrange("p (c e) -> p c e", c=NCH),
                        in_=gate_w.ap().rearrange("(c p) e -> p c e", p=P)).then_inc(p3.ldc, 16)
            for ob in range(4):
                if ob >= 1:
                    g.wait_ge(p3.ldw[0], 16 * ob)
                if ob >= 2:
                    g.wait_ge(p3.mm, 8 * (ob - 1))
                g.dma_start(out=p3.wpb[ob % 2][:, :].rearrange("p (c o) -> p c o", c=NCH),
                            in_=w_proj.ap().rearrange("(c p) o -> p c o", p=P)
                            [:, :, ob * 512:(ob + 1) * 512]).then_inc(p3.ldw[0], 16)
            g.wait_ge(p3.mm, 32)
            for e in range(8):
                for piece, (foff, fw) in enumerate(PIECES):
                    pig = e * 2 + piece
                    par = pig % 2
                    if pig >= 1:
                        g.wait_ge(p3.lde[0], 32 * pig)
                    for (mat, slot) in [(w1, 2 * par), (w3, 2 * par + 1)]:
                        if pig >= 2:
                            pe_, pp_ = divmod(pig - 2, 2)
                            g.wait_ge(p3.mm, mm_gu3(pe_ * 16 + pp_ * 8 + 7))
                        g.dma_start(
                            out=p3.w13b[slot][:, 0:NCH * fw].rearrange(
                                "p (c f) -> p c f", c=NCH),
                            in_=mat[e, :, foff:foff + fw].rearrange("(c p) f -> p c f", p=P)
                        ).then_inc(p3.lde[0], 16)
                for db in range(4):
                    j = e * 4 + db
                    if j >= 1:
                        g.wait_ge(p3.ld2[0], 16 * j)
                    if j >= 2:
                        g.wait_ge(p3.dmm, 48 * (j - 1))
                    g.dma_start(
                        out=p3.w2b[j % 2][:, :].rearrange("p (c d) -> p c d", c=6),
                        in_=w2[e, :, db * 512:(db + 1) * 512].rearrange("(c p) d -> p c d", p=P)
                    ).then_inc(p3.ld2[0], 16)
            g.wait_ge(p3.lnv, 112)
            g.dma_start(out=p4.wgk[:, :].rearrange("p (c e) -> p c e", c=NCH),
                        in_=WgkT.ap().rearrange("(c p) e -> p c e", p=P)).then_inc(p4.ldc, 16)
            g.dma_start(out=p4.cosb[:, :].rearrange("p (tt f) -> p tt f", tt=NTT),
                        in_=rope_cos.ap().rearrange("(tt p) f -> p tt f", p=P)
                        ).then_inc(p4.ldc, 16)
            g.dma_start(out=p4.sinb[:, :].rearrange("p (tt f) -> p tt f", tt=NTT),
                        in_=rope_sin.ap().rearrange("(tt p) f -> p tt f", p=P)
                        ).then_inc(p4.ldc, 16)
            bi = 0
            for mi, mat in enumerate([WqT, WkT, WvT]):
                for ob in range(4):
                    if bi >= 1:
                        g.wait_ge(p4.ldw[0], 16 * bi)
                    if bi >= 2:
                        g.wait_ge(p4.mm, 8 * (bi - 1))
                    g.dma_start(out=p4.wblk[bi % 2][:, :].rearrange("p (c o) -> p c o", c=NCH),
                                in_=mat.ap().rearrange("(c p) o -> p c o", p=P)
                                [:, :, ob * 512:(ob + 1) * 512]).then_inc(p4.ldw[0], 16)
                    if bi == 5:
                        g.wait_ge(p4.sto, 16)
                        g.collective_compute("AllGather", ALU.bypass, replica_groups=RG,
                                             ins=[Agown.ap().opt()], outs=[Ago.ap().opt()]
                                             ).then_inc(cc1, 1)
                    if bi == 7 or bi == 9:
                        i = (bi - 7) // 2
                        g.wait_ge(p4.sto, 16 * (i + 2))
                        g.collective_compute("AllGather", ALU.bypass, replica_groups=RG,
                                             ins=[A1p_own[i].ap().opt()],
                                             outs=[A1p_o[i].ap().opt()]
                                             ).then_inc(ccp[i], 1)
                    bi += 1
            g.wait_ge(p4.sto, 64)
            g.collective_compute("AllGather", ALU.bypass, replica_groups=RG,
                                 ins=[A1p_own[2].ap().opt()], outs=[A1p_o[2].ap().opt()]
                                 ).then_inc(ccp[2], 1)
            g.wait_ge(cc1, 2)
            g.dma_start(out=p5.gsb[:, :].rearrange("p (tt h) -> p tt h", tt=16),
                        in_=Ago.ap().rearrange("(tt p) h -> p tt h", p=P)
                        ).then_inc(p5.gqs, 16)
            g.wait_ge(p5.gsc, 1)
            g.dma_start(out=gdram[:, :], in_=p5.gcum[:, :]).then_inc(p5.gst, 16)
            g.wait_ge(p5.gst, 16)
            rp5 = g.alloc_register("rp5")
            g.load(rp5, isb[0:1, 3:4])
            offv = g.snap(rp5, min_val=0, max_val=16)
            g.dma_start(out=gloc[:, :],
                        in_=gdram[bass.ds(offv, 16), :]).then_inc(p5.gst, 16)
            g.wait_ge(p5.gst, 32)

            def p5_store(pr):
                g.wait_ge(p5.osem, 4 * (pr + 1))
                if pr >= 1:
                    g.wait_ge(p5.sto[0], 16 * pr)
                g.dma_start(out=B1h_own[pr // 8][(pr % 8) * 64:(pr % 8 + 1) * 64, :],
                            in_=p5.osb[pr % 2][:, :]).then_inc(p5.sto[0], 16)

            for pr in range(16):
                if pr >= 1:
                    g.wait_ge(p5.gld[0], 16 * pr)
                if pr >= 2:
                    g.wait_ge(p5.bcs, 4 * (pr - 1))
                g.dma_start(out=p5.grow[pr % 2][:, :],
                            in_=gloc[pr:pr + 1, :]).then_inc(p5.gld[0], 16)
                if pr >= 2:
                    p5_store(pr - 2)
                if pr == 9:
                    g.wait_ge(p5.sto[0], 16 * 8)
                    g.collective_compute("AllGather", ALU.bypass, replica_groups=RG,
                                         ins=[B1h_own[0].ap().opt()],
                                         outs=[B1h_o[0].ap().opt()]
                                         ).then_inc(cc3, 1)
            p5_store(14)
            p5_store(15)
            g.wait_ge(p5.sto[0], 16 * 16)
            g.collective_compute("AllGather", ALU.bypass, replica_groups=RG,
                                 ins=[B1h_own[1].ap().opt()], outs=[B1h_o[1].ap().opt()]
                                 ).then_inc(cc3, 1)
            bi = 0
            for mat in (WgT, WoT):
                for ob in range(8):
                    if bi >= 1:
                        g.wait_ge(p6.ldw[0], 16 * bi)
                    if bi >= 2:
                        g.wait_ge(p6.mm, 8 * (bi - 1))
                    g.dma_start(out=p6.wb[bi % 2][:, :].rearrange("p (c o) -> p c o", c=NCH),
                                in_=mat.ap().rearrange("(c p) o -> p c o", p=P)
                                [:, :, ob * 256:(ob + 1) * 256]).then_inc(p6.ldw[0], 16)
                    bi += 1
            qi = 0
            for th in range(2):
                for hc in range(NHC):
                    if qi >= 1:
                        g.wait_ge(p6.ldq[0], 32 * qi)
                    if qi >= 2:
                        g.wait_ge(p6.gup, 2 * (qi - 1))
                    else:
                        g.wait_ge(p6.mm, NGP6 + NWO6)
                    g.dma_start(out=p6.wqg_b[qi % 2][:, :].rearrange("p (c h) -> p c h", c=NCH),
                                in_=wqgT.ap().rearrange("(c p) h -> p c h", p=P)
                                [:, :, hc * P:(hc + 1) * P]).then_inc(p6.ldq[0], 16)
                    g.dma_start(out=p6.wqu_b[qi % 2][:, :].rearrange("p (c h) -> p c h", c=NCH),
                                in_=wquT.ap().rearrange("(c p) h -> p c h", p=P)
                                [:, :, hc * P:(hc + 1) * P]).then_inc(p6.ldq[0], 16)
                    qi += 1
                for dc in range(NCH):
                    di = th * NCH + dc
                    if di >= 1:
                        g.wait_ge(p6.ldd[0], 16 * di)
                    if di >= 2:
                        g.wait_ge(p6.dse, di - 1)
                    else:
                        g.wait_ge(p6.mm, NGP6 + NWO6)
                    g.dma_start(out=p6.wqd_b[di % 2][:, :].rearrange("p (c d) -> p c d", c=NHC),
                                in_=wqdT.ap().rearrange("(c p) d -> p c d", p=P)
                                [:, :, dc * P:(dc + 1) * P]).then_inc(p6.ldd[0], 16)

        @block.tensor
        def _(t):
            t.wait_ge(cst, 81)
            t.wait_ge(p1.ldx, 32)
            gi = 0
            for ob in range(12):
                t.wait_ge(p1.ldw[0], 16 * (ob + 1))
                for tt in range(NTT):
                    pb = p1.ps[gi % 2]
                    if gi >= 2:
                        t.wait_ge(p1.evs, gi - 1)
                    for c in range(NCH):
                        t.matmul(pb[:, :],
                                 x_f[:, c * TOK + tt * P: c * TOK + (tt + 1) * P],
                                 p1.wblk[ob % 2][:, c * 512:(c + 1) * 512],
                                 start=(c == 0), stop=False)
                    t.matmul(pb[:, :], ones1[:, :],
                             p1.brow[0:1, ob * 512:(ob + 1) * 512],
                             start=False, stop=True).then_inc(p1.mm, 1)
                    gi += 1
            def p2_smms(idx):
                h, gq = divmod(idx, 4)
                rec = p2.sched[idx]
                smm = rec["smm0"]
                for i in range(4):
                    qt = 4 * gq + i
                    for (kc, cw) in chunks_of(qt):
                        if smm >= 4:
                            t.wait_ge(p2.esem, smm - 3)
                        t.matmul(p2.ps_s[smm % 4][:, 0:cw],
                                 p2.qsb[h % 2][:, qt * P:(qt + 1) * P],
                                 p2.ksb[h % 2][:, kc * 512:kc * 512 + cw],
                                 start=True, stop=True).then_inc(p2.ssem, 1)
                        smm += 1

            def p2_tav(idx):
                h, gq = divmod(idx, 4)
                rec = p2.sched[idx]
                t.wait_ge(p2.zsem, rec["zsc1"])
                nkt = 4 * gq + 4

                def p2_tr(kt):
                    ktc = rec["kt0"] + kt
                    i0 = max(0, kt - 4 * gq)
                    if ktc >= 2:
                        t.wait_ge(p2.ptsem, ktc - 1)
                    last = None
                    for i in range(i0, 4):
                        last = t.transpose(p2.pt_ps[ktc % 2][:, i * P:(i + 1) * P],
                                           p2.psb[idx % 2][:, i * T + kt * P: i * T + (kt + 1) * P],
                                           ibfsb[:, :])
                    last.then_inc(p2.tsem, 1)

                def p2_av(kt):
                    ktc = rec["kt0"] + kt
                    i0 = max(0, kt - 4 * gq)
                    t.wait_ge(p2.ptsem, ktc + 1)
                    if kt == 0 and rec["oev0"] > 0:
                        t.wait_ge(p2.osem, rec["oev0"])
                    t.matmul(p2.o_ps[:, i0 * P:512],
                             p2.vsb[h % 2][:, kt * P:(kt + 1) * P],
                             p2.ptsb[ktc % 2][:, i0 * P:512],
                             start=(kt == 0), stop=(kt == nkt - 1)
                             ).then_inc(p2.avsem, 1)

                p2_tr(0)
                for kt in range(1, nkt):
                    p2_tr(kt)
                    p2_av(kt - 1)
                p2_av(nkt - 1)

            for idx in range(32):
                if idx % 4 == 0:
                    t.wait_ge(p2.qld, 48 * (idx // 4 + 1))
                p2_smms(idx)
                if idx >= 1:
                    p2_tav(idx - 1)
            p2_tav(31)
            t.wait_ge(p3.ldx, 32)
            t.wait_ge(p3.ldc, 32)
            gi = 0
            for ob in range(4):
                t.wait_ge(p3.ldw[0], 16 * (ob + 1))
                for tt in range(NTT):
                    pb = p3.ps[gi % 2]
                    if gi >= 2:
                        t.wait_ge(p3.evs, gi - 1)
                    for c in range(NCH):
                        t.matmul(pb[:, :],
                                 p3.ofb[:, c * TOK + tt * P: c * TOK + (tt + 1) * P],
                                 p3.wpb[ob % 2][:, c * 512:(c + 1) * 512],
                                 start=(c == 0), stop=False)
                    t.matmul(pb[:, :], ones1[:, :],
                             p3.bprow[0:1, ob * 512:(ob + 1) * 512],
                             start=False, stop=True).then_inc(p3.mm, 1)
                    gi += 1
            ntr = 0
            for tt in range(NTT):
                t.wait_ge(p3.lnv, 7 * (tt + 1))
                for cg in range(4):
                    if ntr >= 2:
                        t.wait_ge(p3.xfs, ntr - 1)
                    last = None
                    for c4 in range(4):
                        c = cg * 4 + c4
                        last = t.transpose(p3.pxf[ntr % 2][:, c4 * P:(c4 + 1) * P],
                                           stream[:, tt * D + c * P: tt * D + (c + 1) * P],
                                           i32sb[:, :])
                    last.then_inc(p3.trs, 1)
                    ntr += 1
            for tt in range(NTT):
                t.wait_ge(p3.xfs, 4 * (tt + 1))
                if tt > 0:
                    t.wait_ge(p3.gsem, tt)
                for c in range(NCH):
                    t.matmul(p3.pg[0][:, 0:8],
                             x_f[:, c * TOK + tt * P: c * TOK + (tt + 1) * P],
                             p3.gwsb[:, c * 8:(c + 1) * 8],
                             start=(c == 0), stop=(c == NCH - 1)
                             ).then_inc(p3.mm, 1 if c == NCH - 1 else 0)
            t.wait_ge(p3.gsem, 8)
            t.wait_ge(p3.evs, 32)
            t.wait_ge(p3.xfs, 32)
            for e in range(8):
                pi = 0
                def p3_a2tr(gidx):
                    piece_ = (gidx % 16) // 8
                    fw_ = PIECES[piece_][1]
                    t.wait_ge(p3.a2s, gidx + 1)
                    if gidx >= 2:
                        t.wait_ge(p3.a2e, gidx - 1)
                    nt = fw_ // P
                    last = None
                    for c4 in range(nt):
                        last = t.transpose(p3.ptr[gidx % 2][:, c4 * P:(c4 + 1) * P],
                                           p3.a2t[gidx % 2][:, c4 * P:(c4 + 1) * P],
                                           ibfsb[:, :])
                    last.then_inc(p3.trs, 1)

                for piece, (foff, fw) in enumerate(PIECES):
                    pig = e * 2 + piece
                    par = pig % 2
                    t.wait_ge(p3.lde[0], 32 * (pig + 1))
                    for tt in range(NTT):
                        gidx = e * 16 + pi
                        if gidx >= 2:
                            t.wait_ge(p3.sil, gidx - 1)
                            t.wait_ge(p3.a2s, gidx - 1)
                        for c in range(NCH):
                            t.matmul(p3.pg[gidx % 2][:, 0:fw],
                                     x_f[:, c * TOK + tt * P: c * TOK + (tt + 1) * P],
                                     p3.w13b[2 * par][:, c * fw:(c + 1) * fw],
                                     start=(c == 0), stop=(c == NCH - 1))
                        inst = None
                        for c in range(NCH):
                            inst = t.matmul(p3.pu[gidx % 2][:, 0:fw],
                                            x_f[:, c * TOK + tt * P: c * TOK + (tt + 1) * P],
                                            p3.w13b[2 * par + 1][:, c * fw:(c + 1) * fw],
                                            start=(c == 0), stop=(c == NCH - 1))
                        inst.then_inc(p3.mm, 1)
                        if pi >= 1:
                            p3_a2tr(e * 16 + pi - 1)
                        pi += 1
                p3_a2tr(e * 16 + 15)
                for db in range(4):
                    j = e * 4 + db
                    t.wait_ge(p3.ld2[0], 16 * (j + 1))
                    for tt in range(NTT):
                        di = e * 32 + db * 8 + tt
                        if di >= 2:
                            t.wait_ge(p3.dev, di - 1)
                        t.wait_ge(p3.a2e, e * 16 + 8 + tt + 1)
                        inst = None
                        for fc in range(6):
                            inst = t.matmul(p3.ps[di % 2][:, :],
                                            p3.a2f[:, tt * 6 * P + fc * P: tt * 6 * P + (fc + 1) * P],
                                            p3.w2b[j % 2][:, fc * 512:(fc + 1) * 512],
                                            start=(fc == 0), stop=(fc == 5))
                        inst.then_inc(p3.dmm, 6)
            t.wait_ge(p3.lnv, 112)
            t.wait_ge(p3.mm, 168)
            ntr = 0
            for tt in range(NTT):
                for cg in range(4):
                    if ntr >= 2:
                        t.wait_ge(p4.xfs, ntr - 1)
                    last = None
                    for c4 in range(4):
                        c = cg * 4 + c4
                        last = t.transpose(p4.ps[ntr % 2][:, c4 * P:(c4 + 1) * P],
                                           stream[:, tt * D + c * P: tt * D + (c + 1) * P],
                                           i32sb[:, :])
                    last.then_inc(p4.trs, 1)
                    ntr += 1
            t.wait_ge(p4.ldc, 48)
            t.wait_ge(p4.xfs, 32)
            for tt in range(NTT):
                if tt > 0:
                    t.wait_ge(p4.gsig, tt)
                inst = None
                for c in range(NCH):
                    inst = t.matmul(p4.pgk[:, :],
                                    x_f[:, c * TOK + tt * P: c * TOK + (tt + 1) * P],
                                    p4.wgk[:, c * 32:(c + 1) * 32],
                                    start=(c == 0), stop=(c == NCH - 1))
                inst.then_inc(p4.mgk, 1)
            t.wait_ge(p4.ldc, 48)
            gi = 0
            for mi in range(3):
                for ob in range(4):
                    t.wait_ge(p4.ldw[0], 16 * (gi + 1))
                    for tt in range(NTT):
                        grp = gi * 8 + tt
                        pb = p4.ps[grp % 2]
                        if grp >= 2:
                            t.wait_ge(p4.esem, grp - 1)
                        inst = None
                        for c in range(NCH):
                            inst = t.matmul(pb[:, :],
                                            x_f[:, c * TOK + tt * P: c * TOK + (tt + 1) * P],
                                            p4.wblk[gi % 2][:, c * 512:(c + 1) * 512],
                                            start=(c == 0), stop=(c == NCH - 1))
                        inst.then_inc(p4.mm, 1)
                    gi += 1
            t.wait_ge(p4.esem, 96)
            t.wait_ge(p4.gsig, 16)
            t.wait_ge(p5.gqs, 16)
            for kt in range(16):
                if kt >= 1:
                    t.wait_ge(p5.gtr, 2 * kt)
                t.transpose(p5.pbc[0:32, 0:128], p5.gsb[:, kt * 32:(kt + 1) * 32],
                            i32sb[:, :]).then_inc(p5.gtr, 1)
            t.wait_ge(p5.gtr, 32)
            for pr in range(16):
                buf5 = (pr // 2) % 2
                sub5 = (pr % 2) * 64
                t.wait_ge(p5.qld, 48 * (pr // 2 + 1))
                t.wait_ge(p5.gld[0], 16 * (pr + 1))
                if pr >= 1:
                    t.wait_ge(p5.esem, p5.sched[(pr - 1) * 4 + 3]["exp1"])
                    t.wait_ge(p5.bce, 4 * pr)
                last = None
                for j in range(16):
                    last = t.transpose(p5.pbc[:, j:j + 1],
                                       p5.grow[pr % 2][0:1, j * P:(j + 1) * P],
                                       i32sb[0:1, 0:1])
                last.then_inc(p5.gqt, 1)
                t.wait_ge(p5.gqe, pr + 1)
                for j in range(4):
                    if pr > 0 or j > 0:
                        t.wait_ge(p5.bce, 4 * pr + j)
                    t.matmul(p5.pbc[:, :], onec32[:, :],
                             p5.grow[pr % 2][0:1, j * 512:(j + 1) * 512],
                             start=True, stop=True).then_inc(p5.bcs, 1)
                t.wait_ge(p5.bce, 4 * (pr + 1))

                def p5_smms(idx):
                    gq = idx % 4
                    rec = p5.sched[idx]
                    smm = rec["smm0"]
                    for i in range(4):
                        qt = 4 * gq + i
                        for (kc, cw) in chunks_of(qt):
                            if smm >= 4:
                                t.wait_ge(p5.asem, smm - 3)
                            t.matmul(p5.ps_s[smm % 4][:, 0:cw],
                                     p5.qsb[buf5][sub5:sub5 + 64, qt * P:(qt + 1) * P],
                                     p5.ksb[buf5][sub5:sub5 + 64, kc * 512:kc * 512 + cw],
                                     start=True, stop=True).then_inc(p5.ssem, 1)
                            smm += 1

                def p5_tav(idx):
                    gq = idx % 4
                    rec = p5.sched[idx]
                    t.wait_ge(p5.asem, rec["smm1"])
                    nkt = 4 * gq + 4

                    def p5_tr(kt):
                        ktc = rec["kt0"] + kt
                        i0 = max(0, kt - 4 * gq)
                        if ktc >= 2:
                            sem_ = p5.ptsem if ktc % 2 == 0 else p5.ptsem_s
                            t.wait_ge(sem_, (ktc - 2) // 2 + 1)
                        last = None
                        for i in range(i0, 4):
                            last = t.transpose(p5.pt_ps[ktc % 2][:, i * P:(i + 1) * P],
                                               p5.asb[idx % 2][:, i * T + kt * P: i * T + (kt + 1) * P],
                                               ibfsb[:, :])
                        last.then_inc(p5.tsem, 1)

                    def p5_av(kt):
                        ktc = rec["kt0"] + kt
                        i0 = max(0, kt - 4 * gq)
                        sem_ = p5.ptsem if ktc % 2 == 0 else p5.ptsem_s
                        t.wait_ge(sem_, ktc // 2 + 1)
                        if kt == 0 and rec["oev0"] > 0:
                            t.wait_ge(p5.osem, rec["oev0"])
                        t.matmul(p5.o_ps[0:64, i0 * P:512],
                                 p5.vsb[buf5][:, kt * P + sub5: kt * P + sub5 + 64],
                                 p5.ptsb[ktc % 2][:, i0 * P:512],
                                 start=(kt == 0), stop=(kt == nkt - 1)
                                 ).then_inc(p5.avsem, 1)

                    p5_tr(0)
                    for kt in range(1, nkt):
                        p5_tr(kt)
                        p5_av(kt - 1)
                    p5_av(nkt - 1)

                for gq in range(4):
                    p5_smms(pr * 4 + gq)
                    if gq >= 1:
                        p5_tav(pr * 4 + gq - 1)
                p5_tav(pr * 4 + 3)
            t.wait_ge(p5.osem, 64)
            gi = 0
            for ob in range(8):
                t.wait_ge(p6.ldw[0], 16 * (ob + 1))
                for tt in range(NTT):
                    pb = p6.ps[gi % 2]
                    if gi >= 2:
                        t.wait_ge(p6.sil, gi - 1)
                    inst = None
                    for c in range(NCH):
                        inst = t.matmul(pb[:, 0:256],
                                        x_f[:, c * TOK + tt * P: c * TOK + (tt + 1) * P],
                                        p6.wb[ob % 2][:, c * 256:(c + 1) * 256],
                                        start=(c == 0), stop=(c == NCH - 1))
                    inst.then_inc(p6.mm, 1)
                    gi += 1
            ntr = 0
            for tt in range(NTT):
                t.wait_ge(p6.onr, 5 * tt + 5)
                for cg in range(4):
                    if ntr >= 2:
                        t.wait_ge(p6.xfs, ntr - 1)
                    last = None
                    for c4 in range(4):
                        c = cg * 4 + c4
                        last = t.transpose(p6.ptrb[ntr % 2][:, c4 * P:(c4 + 1) * P],
                                           p6.sg_sb[:, tt * D + c * P: tt * D + (c + 1) * P],
                                           ibfsb[:, :])
                    last.then_inc(p6.trs, 1)
                    ntr += 1
            gi = 0
            for ob in range(8):
                bi = 8 + ob
                t.wait_ge(p6.ldw[0], 16 * (bi + 1))
                for tt in range(NTT):
                    grp = NGP6 + gi
                    pb = p6.ps[grp % 2]
                    if gi >= 2:
                        t.wait_ge(p6.evs, gi - 1)
                    else:
                        t.wait_ge(p6.sil, NGP6)
                    t.wait_ge(p6.xfs, 4 * (tt + 1))
                    inst = None
                    for c in range(NCH):
                        inst = t.matmul(pb[:, 0:256],
                                        p6.ogf[:, c * TOK + tt * P: c * TOK + (tt + 1) * P],
                                        p6.wb[bi % 2][:, c * 256:(c + 1) * 256],
                                        start=(c == 0), stop=(c == NCH - 1))
                    inst.then_inc(p6.mm, 1)
                    gi += 1
            ntr = 32
            for tt in range(NTT):
                t.wait_ge(p6.lnv, 7 * (tt + 1))
                for cg in range(4):
                    if ntr >= 34:
                        t.wait_ge(p6.xfs, ntr - 1)
                    else:
                        t.wait_ge(p6.evs, NWO6)
                    last = None
                    for c4 in range(4):
                        c = cg * 4 + c4
                        last = t.transpose(p6.ps[ntr % 2][:, c4 * P:(c4 + 1) * P],
                                           stream[:, tt * D + c * P: tt * D + (c + 1) * P],
                                           i32sb[:, :])
                    last.then_inc(p6.trs, 1)
                    ntr += 1
            for th in range(2):
                tsl = slice(th * 512, (th + 1) * 512)
                t.wait_ge(p6.xfs, 64)
                if th == 1:
                    t.wait_ge(p6.rse, 2)
                inst = None
                for c in range(NCH):
                    inst = t.matmul(p6.prow[0:1, :], nones[:, :],
                                    x_f[:, c * TOK + th * 512: c * TOK + (th + 1) * 512],
                                    start=(c == 0), stop=(c == NCH - 1))
                inst.then_inc(p6.rsm, 1)
                t.wait_ge(p6.rse, 2 * th + 1)
                if th == 1:
                    t.wait_ge(p6.dse, NCH)
                for hc in range(NHC):
                    qi = th * NHC + hc
                    t.wait_ge(p6.ldq[0], 32 * (qi + 1))
                    if qi >= 2:
                        t.wait_ge(p6.gup, 2 * (qi - 1))
                    for c in range(NCH):
                        t.matmul(p6.ps[qi % 2][:, :],
                                 p6.wqg_b[qi % 2][:, c * P:(c + 1) * P],
                                 x_f[:, c * TOK + th * 512: c * TOK + (th + 1) * 512],
                                 start=(c == 0), stop=False)
                    t.matmul(p6.ps[qi % 2][:, :], ones1[:, :], p6.rsrow[0:1, tsl],
                             start=False, stop=True)
                    inst = None
                    for c in range(NCH):
                        t.matmul(p6.pu[qi % 2][:, :],
                                 p6.wqu_b[qi % 2][:, c * P:(c + 1) * P],
                                 x_f[:, c * TOK + th * 512: c * TOK + (th + 1) * 512],
                                 start=(c == 0), stop=False)
                    inst = t.matmul(p6.pu[qi % 2][:, :], ones1[:, :], p6.rsrow[0:1, tsl],
                                    start=False, stop=True)
                    inst.then_inc(p6.mm, 1)
                t.wait_ge(p6.a2s, NHC * (th + 1))
                inst = None
                for hc in range(NHC):
                    inst = t.matmul(p6.prow[0:1, :], nones[:, :],
                                    p6.a2sb[:, hc * 512:(hc + 1) * 512],
                                    start=(hc == 0), stop=(hc == NHC - 1))
                inst.then_inc(p6.rsm, 1)
                t.wait_ge(p6.rse, 2 * th + 2)
                def p6_dtr(pdi):
                    t.wait_ge(p6.dse, pdi + 1)
                    if pdi >= 2:
                        t.wait_ge(p6.dev, pdi - 1)
                    last = None
                    for j in range(4):
                        last = t.transpose(p6.ptrb[pdi % 2][:, j * P:(j + 1) * P],
                                           p6.yfb[pdi % 2][:, j * P:(j + 1) * P],
                                           ibfsb[:, :])
                    last.then_inc(p6.dtr, 1)

                for dc in range(NCH):
                    di = th * NCH + dc
                    t.wait_ge(p6.ldd[0], 16 * (di + 1))
                    if di >= 2:
                        t.wait_ge(p6.dse, di - 1)
                    for hc in range(NHC):
                        t.matmul(p6.pu[di % 2][:, :],
                                 p6.wqd_b[di % 2][:, hc * P:(hc + 1) * P],
                                 p6.a2sb[:, hc * 512:(hc + 1) * 512],
                                 start=(hc == 0), stop=False)
                    inst = t.matmul(p6.pu[di % 2][:, :], ones1[:, :], p6.csrow[0:1, tsl],
                                    start=False, stop=True)
                    inst.then_inc(p6.dmm, 1)
                    if dc >= 1:
                        p6_dtr(di - 1)
                    if dc == NCH - 1:
                        p6_dtr(di)

        @block.vector
        def _(v):
            v.wait_ge(cst, 81)
            gi = 0
            for ob in range(12):
                for tt in range(NTT):
                    v.wait_ge(p1.mm, gi + 1)
                    if ob >= 2:
                        v.wait_ge(p1.so[0], 16 * (ob - 1))
                    v.tensor_copy(p1.ev[ob % 2][:, tt * 512:(tt + 1) * 512],
                                  p1.ps[gi % 2][:, :]).then_inc(p1.evs, 1)
                    gi += 1
            nvself = [0]

            def p2_v_softmax(idx):
                h, gq = divmod(idx, 4)
                rec = p2.sched[idx]
                smm = rec["smm0"]
                for i in range(4):
                    qt = 4 * gq + i
                    chs = chunks_of(qt)
                    for (kc, cw) in chs:
                        if kc == len(chs) - 1:
                            v.wait_ge(p2.ssem, smm + 1)
                            v.tensor_tensor(p2.ps_s[smm % 4][:, cw - P:cw],
                                            p2.ps_s[smm % 4][:, cw - P:cw],
                                            mnegsb[:, :], ALU.add).then_inc(p2.msem, 1)
                        smm += 1
                    v.wait_ge(p2.esem, smm)
                    nkc = len(chs)
                    zq = rec["zsc0"] + i
                    if zq > 0:
                        v.wait_ge(p2.zsem, zq)
                    v.reduce_sum(p2.zinv[:, i:i + 1], p2.zp[:, i * 4:i * 4 + nkc],
                                 AX.X).then_inc(p2.vself, 1)
                    nvself[0] += 1
                    v.wait_ge(p2.vself, nvself[0])
                    v.reciprocal(p2.zinv[:, i:i + 1], p2.zinv[:, i:i + 1]
                                 ).then_inc(p2.vself, 1)
                    nvself[0] += 1
                    v.wait_ge(p2.vself, nvself[0])
                    v.tensor_scalar_mul(p2.psb[idx % 2][:, i * T: i * T + (qt + 1) * P],
                                        p2.psb[idx % 2][:, i * T: i * T + (qt + 1) * P],
                                        p2.zinv[:, i:i + 1]).then_inc(p2.zsem, 1)

            def p2_v_evacs(idx):
                h, gq = divmod(idx, 4)
                rec = p2.sched[idx]
                ktc = rec["kt0"]
                for kt in range(4 * gq + 4):
                    i0 = max(0, kt - 4 * gq)
                    v.wait_ge(p2.tsem, ktc + 1)
                    v.tensor_copy(p2.ptsb[ktc % 2][:, i0 * P:512],
                                  p2.pt_ps[ktc % 2][:, i0 * P:512]).then_inc(p2.ptsem, 1)
                    ktc += 1
                v.wait_ge(p2.avsem, rec["kt1"])
                if h >= 2 and gq == 0:
                    v.wait_ge(p2.sto[0], 16 * (h - 1))
                v.tensor_copy(p2.osb[h % 2][:, gq * 512:(gq + 1) * 512],
                              p2.o_ps[:, :]).then_inc(p2.osem, 1)

            for idx in range(32):
                p2_v_softmax(idx)
                if idx >= 1:
                    p2_v_evacs(idx - 1)
            p2_v_evacs(31)
            gi = 0
            for ob in range(4):
                for tt in range(NTT):
                    v.wait_ge(p3.mm, gi + 1)
                    v.tensor_tensor(stream[:, tt * D + ob * 512: tt * D + (ob + 1) * 512],
                                    p3.ps[gi % 2][:, :],
                                    stream[:, tt * D + ob * 512: tt * D + (ob + 1) * 512],
                                    ALU.add).then_inc(p3.evs, 1)
                    gi += 1

            def ln_vector(ph, tt, lnv_base, lns_base):
                sl = slice(tt * D, (tt + 1) * D)
                v.reduce_sum(sums[:, tt:tt + 1], stream[:, sl], AX.X).then_inc(ph.lnv, 1)
                v.wait_ge(ph.lns, lns_base + 2 * tt + 1)
                v.wait_ge(ph.lnv, lnv_base + 7 * tt + 1)
                v.tensor_scalar_mul(mu[:, tt:tt + 1], sums[:, tt:tt + 1], 1.0 / D
                                    ).then_inc(ph.lnv, 1)
                v.wait_ge(ph.lnv, lnv_base + 7 * tt + 2)
                v.tensor_scalar_mul(varv[:, tt:tt + 1], sq[:, tt:tt + 1], 1.0 / D
                                    ).then_inc(ph.lnv, 1)
                v.wait_ge(ph.lnv, lnv_base + 7 * tt + 3)
                v.scalar_tensor_tensor(varv[:, tt:tt + 1], mu[:, tt:tt + 1], mu[:, tt:tt + 1],
                                       varv[:, tt:tt + 1], ALU.mult, ALU.subtract
                                       ).then_inc(ph.lnv, 1)
                v.wait_ge(ph.lns, lns_base + 2 * tt + 2)
                v.reciprocal(rsd[:, tt:tt + 1], rsd[:, tt:tt + 1]).then_inc(ph.lnv, 1)
                v.wait_ge(ph.lnv, lnv_base + 7 * tt + 5)
                v.tensor_scalar(stream[:, sl], stream[:, sl],
                                mu[:, tt:tt + 1], rsd[:, tt:tt + 1],
                                ALU.subtract, ALU.mult).then_inc(ph.lnv, 2)

            v.wait_ge(p3.evs, 32)
            for tt in range(NTT):
                ln_vector(p3, tt, 0, 0)
            NGOP = 9
            for tt in range(NTT):
                g8 = slice(tt * 8, (tt + 1) * 8)
                v.wait_ge(p3.gsem, tt + 1)
                if tt > 0:
                    v.wait_ge(p3.vch, NGOP * tt)
                v.reduce_max(p3.t2a[:, :], p3.gexp[:, g8], AX.X).then_inc(p3.vch, 1)
                v.wait_ge(p3.vch, NGOP * tt + 1)
                v.tensor_scalar(p3.e8a[:, :], p3.gexp[:, g8], p3.t2a[:, 0:1], None,
                                ALU.is_equal).then_inc(p3.vch, 1)
                v.wait_ge(p3.vch, NGOP * tt + 2)
                v.scalar_tensor_tensor(p3.e8a[:, :], p3.e8a[:, :], -1e30, p3.gexp[:, g8],
                                       ALU.mult, ALU.add).then_inc(p3.vch, 1)
                v.wait_ge(p3.vch, NGOP * tt + 3)
                v.reduce_max(p3.t2b[:, :], p3.e8a[:, :], AX.X).then_inc(p3.vch, 1)
                v.wait_ge(p3.vch, NGOP * tt + 4)
                v.tensor_scalar(p3.e8a[:, :], p3.gexp[:, g8], p3.t2b[:, 0:1], None,
                                ALU.is_ge).then_inc(p3.vch, 1)
                v.wait_ge(p3.vch, NGOP * tt + 5)
                v.tensor_tensor(p3.e8a[:, :], p3.e8a[:, :], p3.gexp[:, g8],
                                ALU.mult).then_inc(p3.vch, 1)
                v.tensor_tensor(p3.t2a[:, :], p3.t2a[:, :], p3.t2b[:, :],
                                ALU.add).then_inc(p3.vch, 1)
                v.wait_ge(p3.vch, NGOP * tt + 7)
                v.reciprocal(p3.t2a[:, :], p3.t2a[:, :]).then_inc(p3.vch, 1)
                v.wait_ge(p3.vch, NGOP * tt + 8)
                v.tensor_scalar_mul(p3.comb[:, g8], p3.e8a[:, :], p3.t2a[:, 0:1]
                                    ).then_inc(p3.vch, 1)
            v.wait_ge(p3.vch, NGOP * NTT)
            for e in range(8):
                pi = 0
                for (foff, fw) in PIECES:
                    for tt in range(NTT):
                        gidx = e * 16 + pi
                        v.wait_ge(p3.sil, gidx + 1)
                        if gidx >= 2:
                            v.wait_ge(p3.trs, 32 + gidx - 1)
                        v.scalar_tensor_tensor(p3.a2t[gidx % 2][:, 0:fw],
                                               p3.pu[gidx % 2][:, 0:fw],
                                               p3.comb[:, tt * 8 + e: tt * 8 + e + 1],
                                               p3.gs[gidx % 2][:, 0:fw],
                                               ALU.mult, ALU.mult).then_inc(p3.a2s, 1)
                        pi += 1
                for db in range(4):
                    for tt in range(NTT):
                        di = e * 32 + db * 8 + tt
                        v.wait_ge(p3.dmm, 6 * (di + 1))
                        v.tensor_tensor(stream[:, tt * D + db * 512: tt * D + (db + 1) * 512],
                                        p3.ps[di % 2][:, :],
                                        stream[:, tt * D + db * 512: tt * D + (db + 1) * 512],
                                        ALU.add).then_inc(p3.dev, 1)
            v.wait_ge(p3.dev, 256)
            for tt in range(NTT):
                ln_vector(p3, tt, 56, 16)
            nv4 = [0]
            v.wait_ge(p4.ldc, 48)
            for mi in range(2):
                for tt in range(NTT):
                    v.wait_ge(p4.esem, mi * 32 + 24 + tt + 1)
                    v.wait_ge(p4.vch, nv4[0])
                    st3 = p4.stage[:, tt * D:(tt + 1) * D].rearrange("p (h f) -> p h f", h=32)
                    x1 = st3[:, :, 0:32]
                    x2v = st3[:, :, 32:64]
                    ccb = p4.cosb[:, tt * 64: tt * 64 + 32].unsqueeze(1).broadcast_to([P, 32, 32])
                    ssb = p4.sinb[:, tt * 64: tt * 64 + 32].unsqueeze(1).broadcast_to([P, 32, 32])
                    t3 = [p4.tmp[i][:, :].rearrange("p (h f) -> p h f", h=32) for i in range(4)]
                    v.tensor_tensor(t3[0], x1, ccb, ALU.mult).then_inc(p4.vch, 1); nv4[0] += 1
                    v.tensor_tensor(t3[1], x2v, ssb, ALU.mult).then_inc(p4.vch, 1); nv4[0] += 1
                    v.tensor_tensor(t3[2], x1, ssb, ALU.mult).then_inc(p4.vch, 1); nv4[0] += 1
                    v.tensor_tensor(t3[3], x2v, ccb, ALU.mult).then_inc(p4.vch, 1); nv4[0] += 1
                    v.wait_ge(p4.vch, nv4[0])
                    v.tensor_tensor(x1, t3[0], t3[1], ALU.subtract).then_inc(p4.vch, 1); nv4[0] += 1
                    v.tensor_tensor(x2v, t3[2], t3[3], ALU.add).then_inc(p4.vch, 1); nv4[0] += 1
            v.wait_ge(p5.gtr, 32)
            v.tensor_tensor_scan(p5.gcum[:, :], p5.gf[:, :], p5.gf[:, :], 0.0,
                                 ALU.add, ALU.bypass).then_inc(p5.gsc, 1)
            nv5 = [0]
            v.wait_ge(p5.gst, 16)

            def p5_v_amul(idx):
                gq = idx % 4
                rec = p5.sched[idx]
                smm = rec["smm0"]
                if idx >= 2:
                    v.wait_ge(p5.tsem, p5.sched[idx - 2]["kt1"])
                for i in range(4):
                    qt = 4 * gq + i
                    chs = chunks_of(qt)
                    for (kc, cw) in chs:
                        isdiag = (kc == len(chs) - 1)
                        v.wait_ge(p5.esem, smm + 1)
                        if isdiag:
                            v.tensor_tensor(p5.esb[smm % 2][:, cw - P:cw],
                                            p5.esb[smm % 2][:, cw - P:cw],
                                            mbfsb[:, :], ALU.mult).then_inc(p5.vsf, 1)
                            nv5[0] += 1
                            v.wait_ge(p5.vsf, nv5[0])
                        v.wait_ge(p5.ssem, smm + 1)
                        v.scalar_tensor_tensor(
                            p5.asb[idx % 2][:, i * T + kc * 512: i * T + kc * 512 + cw],
                            p5.ps_s[smm % 4][:, 0:cw], 0.125,
                            p5.esb[smm % 2][:, 0:cw],
                            ALU.mult, ALU.mult).then_inc(p5.asem, 1)
                        smm += 1

            def p5_v_evacs(idx):
                pr, gq = divmod(idx, 4)
                rec = p5.sched[idx]
                ktc = rec["kt0"]
                for kt in range(4 * gq + 4):
                    i0 = max(0, kt - 4 * gq)
                    if ktc % 2 == 0:
                        v.wait_ge(p5.tsem, ktc + 1)
                        v.tensor_copy(p5.ptsb[ktc % 2][:, i0 * P:512],
                                      p5.pt_ps[ktc % 2][:, i0 * P:512]).then_inc(p5.ptsem, 1)
                    ktc += 1
                v.wait_ge(p5.avsem, rec["kt1"])
                if pr >= 2 and gq == 0:
                    v.wait_ge(p5.sto[0], 16 * (pr - 1))
                v.tensor_copy(p5.osb[pr % 2][:, gq * 512:(gq + 1) * 512],
                              p5.o_ps[0:64, :]).then_inc(p5.osem, 1)

            for idx5 in range(64):
                if idx5 >= 1 and idx5 % 4 == 0:
                    p5_v_evacs(idx5 - 1)
                p5_v_amul(idx5)
                if idx5 >= 1 and idx5 % 4 != 0:
                    p5_v_evacs(idx5 - 1)
            p5_v_evacs(63)

            v.wait_ge(p3.lnv, 112)
            for tt in range(NTT):
                v.wait_ge(p6.ldo[0], 16 * (tt + 1))
                v.wait_ge(p6.onr, 5 * tt)
                v.wait_ge(p6.lns, 2 * tt + 1)
                o3 = p6.otb[tt % 2][:, :].rearrange("p (h f) -> p h f", h=32)
                v.reduce_sum(p6.hvar[:, :].unsqueeze(2),
                             scr[:, :].rearrange("p (h f) -> p h f", h=32),
                             AX.X).then_inc(p6.onr, 1)
                v.wait_ge(p6.onr, 5 * tt + 1)
                v.wait_ge(p6.lns, 2 * tt + 2)
                v.reciprocal(p6.hvar[:, :], p6.hvar[:, :]).then_inc(p6.onr, 1)
                v.wait_ge(p6.onr, 5 * tt + 2)
                v.tensor_tensor(p6.otb[tt % 2][:, :].rearrange("p (h f) -> p h f", h=32),
                                o3, p6.hvar[:, :].unsqueeze(2).broadcast_to([P, 32, 64]),
                                ALU.mult).then_inc(p6.onr, 1)
                v.wait_ge(p6.onr, 5 * tt + 3)
                v.wait_ge(p6.sil, NGP6)
                v.tensor_tensor(p6.sg_sb[:, tt * D:(tt + 1) * D],
                                p6.sg_sb[:, tt * D:(tt + 1) * D],
                                p6.otb[tt % 2][:, :], ALU.mult).then_inc(p6.onr, 2)
            gi = 0
            for ob in range(8):
                for tt in range(NTT):
                    grp = NGP6 + gi
                    v.wait_ge(p6.mm, grp + 1)
                    v.tensor_tensor(stream[:, tt * D + ob * 256: tt * D + (ob + 1) * 256],
                                    p6.ps[grp % 2][:, 0:256],
                                    stream[:, tt * D + ob * 256: tt * D + (ob + 1) * 256],
                                    ALU.add).then_inc(p6.evs, 1)
                    gi += 1
            v.wait_ge(p6.evs, NWO6)
            for tt in range(NTT):
                ln_vector(p6, tt, 0, 16)
            for th in range(2):
                for hc in range(NHC):
                    qi = th * NHC + hc
                    v.wait_ge(p6.gup, 2 * (qi + 1))
                    v.tensor_tensor(p6.a2sb[:, hc * 512:(hc + 1) * 512],
                                    p6.sgh[qi % 2][:, :],
                                    p6.a2sb[:, hc * 512:(hc + 1) * 512],
                                    ALU.mult).then_inc(p6.a2s, 1)
                for dc in range(NCH):
                    di = th * NCH + dc
                    v.wait_ge(p6.dtr, di + 1)
                    for j in range(4):
                        tt = th * 4 + j
                        inst = v.tensor_tensor(
                            stream[:, tt * D + dc * P: tt * D + (dc + 1) * P],
                            p6.ptrb[di % 2][:, j * P:(j + 1) * P],
                            stream[:, tt * D + dc * P: tt * D + (dc + 1) * P],
                            ALU.add)
                        if j == 3:
                            inst.then_inc(p6.dev, 1)
            v.wait_ge(p6.dev, 32)
            for tt in range(NTT):
                ln_vector(p6, tt, 56, 32)

        @block.scalar
        def _(s):
            s.wait_ge(cst, 81)
            SCALE2 = 128.0 ** -0.5
            for h in range(8):
                for gq in range(4):
                    idx = h * 4 + gq
                    rec = p2.sched[idx]
                    if idx > 1:
                        prev = p2.sched[idx - 2]
                        s.wait_ge(p2.tsem, prev["kt1"])
                    if idx > 0:
                        s.wait_ge(p2.zsem, p2.sched[idx - 1]["zsc1"])
                    smm = rec["smm0"]
                    mk = rec["mask0"]
                    for i in range(4):
                        qt = 4 * gq + i
                        chs = chunks_of(qt)
                        for (kc, cw) in chs:
                            if kc == len(chs) - 1:
                                mk += 1
                                s.wait_ge(p2.msem, mk)
                            else:
                                s.wait_ge(p2.ssem, smm + 1)
                            s.activation(p2.psb[idx % 2][:, i * T + kc * 512: i * T + kc * 512 + cw],
                                         p2.ps_s[smm % 4][:, 0:cw], AF.Exp,
                                         bias=shcol[:, :], scale=SCALE2,
                                         accum_out=p2.zp[:, i * 4 + kc:i * 4 + kc + 1]
                                         ).then_inc(p2.esem, 1)
                            smm += 1
            def ln_scalar(ph, tt, lnv_base, lns_base):
                sl = slice(tt * D, (tt + 1) * D)
                s.wait_ge(ph.lns, lns_base + 2 * tt)
                s.activation(scr[:, :], stream[:, sl], AF.Square,
                             accum_out=sq[:, tt:tt + 1]).then_inc(ph.lns, 1)
                s.wait_ge(ph.lnv, lnv_base + 7 * tt + 4)
                s.activation(rsd[:, tt:tt + 1], varv[:, tt:tt + 1], AF.Sqrt,
                             bias=epsln[:, :], scale=-1.0).then_inc(ph.lns, 1)

            s.wait_ge(p3.evs, 32)
            for tt in range(NTT):
                ln_scalar(p3, tt, 0, 0)
            ntr = 0
            for tt in range(NTT):
                for cg in range(4):
                    s.wait_ge(p3.trs, ntr + 1)
                    s.copy(x_f[:, :].rearrange("p (c t) -> p c t", c=NCH)
                           [:, cg * 4:(cg + 1) * 4, tt * P:(tt + 1) * P],
                           p3.pxf[ntr % 2][:, :].rearrange("p (c4 t) -> p c4 t", c4=4)
                           ).then_inc(p3.xfs, 1)
                    ntr += 1
            for tt in range(NTT):
                s.wait_ge(p3.mm, 32 + tt + 1)
                s.activation(p3.gexp[:, tt * 8:(tt + 1) * 8], p3.pg[0][:, 0:8],
                             AF.Exp).then_inc(p3.gsem, 1)
            for e in range(8):
                pi = 0
                for (foff, fw) in PIECES:
                    for tt in range(NTT):
                        gidx = e * 16 + pi
                        s.wait_ge(p3.mm, mm_gu3(gidx))
                        if gidx >= 2:
                            s.wait_ge(p3.a2s, gidx - 1)
                        s.activation(p3.gs[gidx % 2][:, 0:fw], p3.pg[gidx % 2][:, 0:fw],
                                     AF.Silu).then_inc(p3.sil, 1)
                        s.wait_ge(p3.trs, 32 + gidx + 1)
                        if e >= 1 and pi == 0:
                            s.wait_ge(p3.dmm, 192 * e)
                        nt = fw // P
                        s.copy(p3.a2f[:, tt * 6 * P + (foff // P) * P:
                                      tt * 6 * P + (foff // P) * P + nt * P],
                               p3.ptr[gidx % 2][:, 0:nt * P]).then_inc(p3.a2e, 1)
                        pi += 1
            s.wait_ge(p3.dev, 256)
            for tt in range(NTT):
                ln_scalar(p3, tt, 56, 16)
            ntr = 0
            for tt in range(NTT):
                for cg in range(4):
                    s.wait_ge(p4.trs, ntr + 1)
                    s.copy(x_f[:, :].rearrange("p (c t) -> p c t", c=NCH)
                           [:, cg * 4:(cg + 1) * 4, tt * P:(tt + 1) * P],
                           p4.ps[ntr % 2][:, :].rearrange("p (c4 t) -> p c4 t", c4=4)
                           ).then_inc(p4.xfs, 1)
                    ntr += 1
            for tt in range(NTT):
                s.wait_ge(p4.mgk, tt + 1)
                s.activation(p4.gks[:, tt * 32:(tt + 1) * 32], p4.pgk[:, :],
                             AF.Sigmoid).then_inc(p4.gsig, 1)
            for tt in range(NTT):
                s.wait_ge(p4.gsig, NTT + tt)
                s.activation(p4.gks[:, tt * 32:(tt + 1) * 32],
                             p4.gks[:, tt * 32:(tt + 1) * 32],
                             AF.Ln).then_inc(p4.gsig, 1)
            gi = 0
            for mi in range(3):
                for ob in range(4):
                    for tt in range(NTT):
                        grp = gi * 8 + tt
                        s.wait_ge(p4.mm, grp + 1)
                        if ob == 0 and tt == 0:
                            s.wait_ge(p4.sto, 16 * (mi + 1))
                        s.copy(p4.stage[:, tt * D + ob * 512: tt * D + (ob + 1) * 512],
                               p4.ps[grp % 2][:, :]).then_inc(p4.esem, 1)
                    gi += 1
            for kt in range(16):
                s.wait_ge(p5.gtr, 2 * kt + 1)
                s.copy(p5.gf[0:32, kt * P:(kt + 1) * P],
                       p5.pbc[0:32, 0:P]).then_inc(p5.gtr, 1)
            def p5_s_evac(pidx):
                prec = p5.sched[pidx]
                pgq = pidx % 4
                pktc = prec["kt0"]
                for pkt in range(4 * pgq + 4):
                    pi0 = max(0, pkt - 4 * pgq)
                    if pktc % 2 == 1:
                        s.wait_ge(p5.tsem, pktc + 1)
                        s.copy(p5.ptsb[pktc % 2][:, pi0 * P:512],
                               p5.pt_ps[pktc % 2][:, pi0 * P:512]
                               ).then_inc(p5.ptsem_s, 1)
                    pktc += 1

            for pr in range(16):
                if pr >= 1:
                    p5_s_evac(pr * 4 - 1)
                s.wait_ge(p5.gqt, pr + 1)
                if pr >= 1:
                    s.wait_ge(p5.esem, p5.sched[(pr - 1) * 4 + 3]["exp1"])
                s.activation(p5.gq16[:, :], p5.pbc[:, 0:16], AF.Copy,
                             scale=1.0 / 16.0).then_inc(p5.gqe, 1)
                for j in range(4):
                    s.wait_ge(p5.bcs, 4 * pr + j + 1)
                    s.copy(p5.gkbc[:, j * 512:(j + 1) * 512], p5.pbc[:, :]
                           ).then_inc(p5.bce, 1)
                for gq in range(4):
                    rec = p5.sched[pr * 4 + gq]
                    smm = rec["smm0"]
                    if gq == 0:
                        s.wait_ge(p5.bce, 4 * (pr + 1))
                    for i in range(4):
                        qt = 4 * gq + i
                        for (kc, cw) in chunks_of(qt):
                            if smm >= 2:
                                s.wait_ge(p5.asem, smm - 1)
                            s.activation(p5.esb[smm % 2][:, 0:cw],
                                         p5.gkbc[:, kc * 512:kc * 512 + cw],
                                         AF.Exp, bias=p5.gq16[:, qt:qt + 1],
                                         scale=-1.0 / 16.0).then_inc(p5.esem, 1)
                            smm += 1
                    if gq >= 1:
                        p5_s_evac(pr * 4 + gq - 1)
            p5_s_evac(63)
            s.wait_ge(p3.lns, 32)
            s.wait_ge(p6.ldm, 48)
            gi = 0
            for ob in range(8):
                for tt in range(NTT):
                    s.wait_ge(p6.mm, gi + 1)
                    s.activation(p6.sg_sb[:, tt * D + ob * 256: tt * D + (ob + 1) * 256],
                                 p6.ps[gi % 2][:, 0:256], AF.Silu).then_inc(p6.sil, 1)
                    gi += 1
            for tt in range(NTT):
                s.wait_ge(p6.ldo[0], 16 * (tt + 1))
                if tt >= 1:
                    s.wait_ge(p6.onr, 5 * (tt - 1) + 1)
                s.activation(scr[:, :], p6.otb[tt % 2][:, :], AF.Square).then_inc(p6.lns, 1)
                s.wait_ge(p6.onr, 5 * tt + 1)
                s.activation(p6.hvar[:, :], p6.hvar[:, :], AF.Sqrt,
                             bias=epsrm[:, :], scale=1.0 / 64.0).then_inc(p6.lns, 1)
            ntr = 0
            for tt in range(NTT):
                for cg in range(4):
                    s.wait_ge(p6.trs, ntr + 1)
                    s.copy(p6.ogf[:, :].rearrange("p (c t) -> p c t", c=NCH)
                           [:, cg * 4:(cg + 1) * 4, tt * P:(tt + 1) * P],
                           p6.ptrb[ntr % 2][:, 0:512].rearrange("p (c4 t) -> p c4 t", c4=4)
                           ).then_inc(p6.xfs, 1)
                    ntr += 1
            s.wait_ge(p6.evs, NWO6)
            for tt in range(NTT):
                ln_scalar(p6, tt, 0, 16)
            for tt in range(NTT):
                for cg in range(4):
                    s.wait_ge(p6.trs, ntr + 1)
                    s.copy(x_f[:, :].rearrange("p (c t) -> p c t", c=NCH)
                           [:, cg * 4:(cg + 1) * 4, tt * P:(tt + 1) * P],
                           p6.ps[ntr % 2][:, :].rearrange("p (c4 t) -> p c4 t", c4=4)
                           ).then_inc(p6.xfs, 1)
                    ntr += 1
            for th in range(2):
                tsl = slice(th * 512, (th + 1) * 512)
                s.wait_ge(p6.rsm, 2 * th + 1)
                s.copy(p6.rsrow[0:1, tsl], p6.prow[0:1, :]).then_inc(p6.rse, 1)
                for hc in range(NHC):
                    qi = th * NHC + hc
                    s.wait_ge(p6.mm, NGP6 + NWO6 + qi + 1)
                    if qi >= 2:
                        s.wait_ge(p6.a2s, qi - 1)
                    s.activation(p6.sgh[qi % 2][:, :], p6.ps[qi % 2][:, :], AF.Silu,
                                 scale=p6.sgc[:, hc:hc + 1]).then_inc(p6.gup, 1)
                    s.activation(p6.a2sb[:, hc * 512:(hc + 1) * 512], p6.pu[qi % 2][:, :],
                                 AF.Copy, scale=p6.suc[:, hc:hc + 1]).then_inc(p6.gup, 1)
                s.wait_ge(p6.rsm, 2 * th + 2)
                s.copy(p6.csrow[0:1, tsl], p6.prow[0:1, :]).then_inc(p6.rse, 1)
                for dc in range(NCH):
                    di = th * NCH + dc
                    s.wait_ge(p6.dmm, di + 1)
                    if di >= 2:
                        s.wait_ge(p6.dtr, di - 1)
                    s.activation(p6.yfb[di % 2][:, :], p6.pu[di % 2][:, :], AF.Copy,
                                 scale=p6.sdc[:, dc:dc + 1]).then_inc(p6.dse, 1)
            s.wait_ge(p6.dev, 32)
            for tt in range(NTT):
                ln_scalar(p6, tt, 56, 32)

    ctx.close()
    return nc


from concourse.bass_utils import run_bass_kernel_spmd
import ml_dtypes

_BF = ml_dtypes.bfloat16
_cc = np.ascontiguousarray


def _planar_perm():
    perm = np.empty(2048, np.int64)
    for h in range(32):
        for i in range(64):
            perm[h * 64 + i] = h * 64 + (2 * i if i < 32 else 2 * (i - 32) + 1)
    return perm


def kernel(L0_attn_x, L0_attn_w_attn, L0_attn_b_attn, L0_attn_w_proj, L0_attn_b_proj,
           L0_ffn_gate_w, L0_ffn_w1, L0_ffn_w2, L0_ffn_w3,
           L1_attn_Wq, L1_attn_Wk, L1_attn_Wv, L1_attn_Wg, L1_attn_Wgk, L1_attn_Wo,
           L1_ffn_wq_gate, L1_ffn_s_gate, L1_ffn_wq_up, L1_ffn_s_up,
           L1_ffn_wq_down, L1_ffn_s_down):
    x = np.asarray(L0_attn_x, np.float32)
    perm = _planar_perm()
    weights = {
        "w_attn": np.asarray(L0_attn_w_attn, np.float32),
        "b_attn": np.asarray(L0_attn_b_attn, np.float32),
        "w_proj": np.asarray(L0_attn_w_proj, np.float32),
        "b_proj": np.asarray(L0_attn_b_proj, np.float32),
        "gate_w": np.asarray(L0_ffn_gate_w, np.float32),
        "w1": np.asarray(L0_ffn_w1, np.float32),
        "w2": np.asarray(L0_ffn_w2, np.float32),
        "w3": np.asarray(L0_ffn_w3, np.float32),
        "WqT": _cc(np.asarray(L1_attn_Wq, np.float32).T[:, perm]),
        "WkT": _cc(np.asarray(L1_attn_Wk, np.float32).T[:, perm]),
        "WvT": _cc(np.asarray(L1_attn_Wv, np.float32).T),
        "WgT": _cc(np.asarray(L1_attn_Wg, np.float32).T),
        "WgkT": _cc(np.asarray(L1_attn_Wgk, np.float32).T),
        "WoT": _cc(np.asarray(L1_attn_Wo, np.float32).T),
        "wqgT": _cc(np.asarray(L1_ffn_wq_gate, np.int32).T),
        "wquT": _cc(np.asarray(L1_ffn_wq_up, np.int32).T),
        "wqdT": _cc(np.asarray(L1_ffn_wq_down, np.int32).T),
        "sg2d": _cc(np.asarray(L1_ffn_s_gate, np.float32).reshape(44, 128).T),
        "su2d": _cc(np.asarray(L1_ffn_s_up, np.float32).reshape(44, 128).T),
        "sd2d": _cc(np.asarray(L1_ffn_s_down, np.float32).reshape(16, 128).T),
    }
    id32 = np.eye(128, dtype=np.float32)
    consts = {
        "id32": id32,
        "idbf": id32.astype(_BF),
        "maskneg": np.triu(np.full((128, 128), -1e30, np.float32), 1),
        "maskbf": np.tril(np.ones((128, 128), np.float32)).astype(_BF),
    }
    inv = 1.0 / 10000.0 ** (np.arange(0, 64, 2) / 64.0)
    in_maps = []
    for c in range(8):
        b, half = c // 2, c % 2
        sl = slice(half * 1024, half * 1024 + 1024)
        pos = np.arange(half * 1024, half * 1024 + 1024, dtype=np.float64)
        fr = pos[:, None] * inv[None, :]
        cosm = np.concatenate([np.cos(fr), np.cos(fr)], 1).astype(np.float32)
        sinm = np.concatenate([np.sin(fr), np.sin(fr)], 1).astype(np.float32)
        m = {"x_f": _cc(x[b, sl].T), "x_t": _cc(x[b, sl]),
             "coreinfo": np.array([[1024 * half, 1024 * half, 1024 * half,
                                    16 * half]], np.int32),
             "rope_cos": cosm, "rope_sin": sinm}
        m.update(weights)
        m.update(consts)
        in_maps.append(m)
    nc = build_fused()
    res = run_bass_kernel_spmd(nc, in_maps, core_ids=list(range(8)))
    out = np.empty((4, 2048, 2048), np.float32)
    for c in range(8):
        b, half = c // 2, c % 2
        out[b, half * 1024:half * 1024 + 1024] = res.results[c]["y"]
    return out
